# version 06
# baseline (speedup 1.0000x reference)
import numpy as np
import concourse.bass as bass
import concourse.mybir as mybir
from concourse.bass_utils import run_bass_kernel_spmd

F32 = mybir.dt.float32
BF16 = mybir.dt.bfloat16
AF = mybir.ActivationFunctionType
ALU = mybir.AluOpType
AX = mybir.AxisListType

D = 1024
FF = 4096
KT = 8
FT = 32
TPC = 2048
CH = 256
NCH = TPC // CH
JT = CH // 128
NT = TPC // 128
EPS = 1e-5


class Buf:
    __slots__ = ("w", "r")

    def __init__(self):
        self.w = None
        self.r = []


class Bld:

    ENG_SECT = {"t": "tensor", "v": "vector", "s": "scalar", "sy": "sync", "g": "gpsimd"}

    def __init__(self):
        self.ops = {e: [] for e in self.ENG_SECT}
        self.n = {k: 0 for k in ("t", "v", "s", "qx", "qw", "qo")}
        self.waited = {}
        self.dummy_fn = {}
        self.dmy_buf = {"v": Buf(), "s": Buf()}

    def _wait(self, eng, dep):
        stream, val = dep
        if stream == eng:
            if eng == "t":
                return
        key = (eng, stream)
        if self.waited.get(key, 0) >= val:
            return
        self.waited[key] = val
        self.ops[eng].append(("wait", stream, val))

    def op(self, eng, fn, reads=(), writes=(), queue=None):
        if eng in ("v", "s") and self.dummy_fn.get(eng) is not None:
            last = (eng, self.n[eng])
            if any(b.w == last for b in reads):
                self.op(eng, self.dummy_fn[eng], reads=(), writes=(self.dmy_buf[eng],))
        for b in reads:
            if b.w is not None:
                self._wait(eng, b.w)
        for b in writes:
            if b.w is not None:
                self._wait(eng, b.w)
            for rd in b.r:
                self._wait(eng, rd)
        if queue is None:
            self.n[eng] += 1
            tok = (eng, self.n[eng])
            self.ops[eng].append(("do", fn, eng, 1))
        else:
            if self.n[queue] > 0:
                self._wait(eng, (queue, 16 * self.n[queue]))
            self.n[queue] += 1
            tok = (queue, 16 * self.n[queue])
            self.ops[eng].append(("do", fn, queue, 16))
        for b in reads:
            b.r.append(tok)
        for b in writes:
            b.w = tok
            b.r = []

    def raw_wait(self, eng, stream, val):
        key = (eng, stream)
        if self.waited.get(key, 0) >= val:
            return
        self.waited[key] = val
        self.ops[eng].append(("wait", stream, val))


def build_nc():
    nc = bass.Bass("TRN2", target_bir_lowering=False)

    xh = nc.dram_tensor("x", [NT, 128, D], F32, kind="ExternalInput")
    wvt_h = nc.dram_tensor("wvt", [128, KT, D], F32, kind="ExternalInput")
    wout_h = nc.dram_tensor("wout", [128, KT, D], F32, kind="ExternalInput")
    bv_h = nc.dram_tensor("bvp", [128, KT], F32, kind="ExternalInput")
    boutb_h = nc.dram_tensor("boutb", [128, D], F32, kind="ExternalInput")
    gwt_h = nc.dram_tensor("gwt", [128, KT, 8], F32, kind="ExternalInput")
    gb_h = nc.dram_tensor("gb", [8, 1], F32, kind="ExternalInput")
    wit_h = nc.dram_tensor("wit", [128, KT, FF], F32, kind="ExternalInput")
    wib_h = nc.dram_tensor("wibp", [128, FT], F32, kind="ExternalInput")
    wot_h = nc.dram_tensor("wot", [128, FT, D], F32, kind="ExternalInput")
    wobb_h = nc.dram_tensor("wobb", [128, D], F32, kind="ExternalInput")
    a2t_h = nc.dram_tensor("a2t", [128, KT, 32], F32, kind="ExternalInput")
    b2_h = nc.dram_tensor("b2", [32, FF], F32, kind="ExternalInput")
    eye_h = nc.dram_tensor("eye", [128, 128], F32, kind="ExternalInput")
    x1s_h = nc.dram_tensor("x1s", [NT, 128, D], F32, kind="Internal")
    yh = nc.dram_tensor("y", [NT, 128, D], F32, kind="ExternalOutput")

    layout = []

    def L(name, shape, dtype):
        layout.append((name, shape, dtype))
        return name

    static = [
        ("xsb0", [128, JT, D], F32), ("xsb1", [128, JT, D], F32),
        ("htmp0", [128, JT, D], BF16), ("htmp1", [128, JT, D], BF16),
        ("hT", [128, KT, CH], BF16),
        ("aotmp", [128, D], F32),
        ("trash", [128, D], F32),
        ("eyebf", [128, 128], BF16),
        ("stats", [128, 64], F32),
        ("lg", [8, CH], BF16),
        ("lgT", [128, 16], BF16),
        ("maskT", [128, JT, 32], BF16),
        ("mask32", [32, CH], F32),
        ("xam", [32, CH], BF16),
        ("bv", [128, KT], F32),
        ("gateb", [8, 16], F32),
        ("wib", [128, FT], F32),
        ("boutb", [128, D], F32),
        ("a2t", [128, KT, 32], BF16),
        ("b2", [32, FF], BF16),
        ("gwt", [128, KT, 8], BF16),
        ("if0", [128, CH], BF16), ("if1", [128, CH], BF16),
        ("dmyv", [128, 64], F32), ("dmys", [128, 64], F32),
        ("aoraw", [128, JT, D], F32),
        ("wiT", [128, KT, FF], BF16),
        ("woTlo", [128, 16, D], BF16),
    ]
    dt_size = {F32: 4, BF16: 2}
    off = 0
    offs = {}
    for name, shape, dtype in static:
        per_part = int(np.prod(shape[1:])) * dt_size[dtype]
        offs[name] = off
        off += (per_part + 31) // 32 * 32
    alias_base = off
    alias = [
        ("wvT", [128, KT, D], BF16, 0),
        ("woutT", [128, KT, D], BF16, 16384),
        ("vT", [128, KT, CH], BF16, 32768),
        ("woThi", [128, 16, D], BF16, 0),
        ("wobb", [128, D], F32, 32768),
    ]
    total = alias_base + 40960
    arena = nc.alloc_sbuf_tensor("arena", [128, total], mybir.dt.uint8)
    base = nc.lookup_mloc(arena).addr

    sb = {}
    for name, shape, dtype in static:
        sb[name] = nc.alloc_sbuf_tensor_at(name, shape, dtype, offset=base + offs[name])
    for name, shape, dtype, aoff in alias:
        sb[name] = nc.alloc_sbuf_tensor_at(name, shape, dtype, offset=base + alias_base + aoff)

    pT0 = nc.alloc_psum_tensor("pT0", [128, 1024], BF16)
    pT1 = nc.alloc_psum_tensor("pT1", [128, 1024], BF16)
    pV0 = nc.alloc_psum_tensor("pV0", [128, 512], F32)
    pV1 = nc.alloc_psum_tensor("pV1", [128, 512], F32)
    pO0 = nc.alloc_psum_tensor("pO0", [128, 1024], F32)
    pO1 = nc.alloc_psum_tensor("pO1", [128, 1024], F32)
    pT = [pT0, pT1]
    pV = [pV0, pV1]
    pO = [pO0, pO1]

    B = Bld()
    B.dummy_fn["v"] = lambda v: v.tensor_copy(sb["dmyv"][:, 0:32], sb["dmyv"][:, 32:64])
    B.dummy_fn["s"] = lambda s: s.copy(sb["dmys"][:, 0:32], sb["dmys"][:, 32:64])
    xt = [[Buf() for _ in range(JT)] for _ in range(2)]
    ht = [[Buf() for _ in range(JT)] for _ in range(2)]
    hTg = [Buf(), Buf()]
    vTb = [Buf() for _ in range(KT)]
    bpT = [Buf(), Buf()]
    bpV = [Buf(), Buf()]
    bpO = [Buf(), Buf()]
    bwv = [Buf(), Buf()]; bwo = Buf(); bgwt = Buf(); ba2t = Buf(); bb2 = Buf(); beye = Buf()
    bwiT = [Buf() for _ in range(4)]
    bwoT = [Buf() for _ in range(4)]
    bbv = Buf(); bboutb = Buf(); bgb = Buf(); bwib = Buf(); bwobb = Buf()
    bst = {k: Buf() for k in ("s", "q", "m", "e", "ms", "v", "sd", "r",
                              "s2", "q2", "m2", "e2", "ms2", "v2", "sd2", "r2", "mx")}
    blg = Buf(); blgT = Buf(); bmaskT = [Buf() for _ in range(JT)]
    bmask32 = Buf(); bxam = Buf()
    bif = [Buf(), Buf()]
    bx1s = [Buf() for _ in range(NT)]
    btrash = Buf()

    SC = {"s": 0, "q": 2, "m": 4, "e": 6, "ms": 8, "v": 10, "sd": 12, "r": 14,
          "s2": 16, "q2": 17, "m2": 18, "e2": 19, "ms2": 20, "v2": 21, "sd2": 22,
          "r2": 23, "mx": 24}

    st = sb["stats"]

    def stc(key, j=0, n=1):
        c = SC[key] + j
        return st[:, c:c + n]

    B.op("v", lambda v: v.memset(sb["dmyv"][:, :], 0.0), writes=[B.dmy_buf["v"]])
    B.op("s", lambda s: s.memzero(sb["dmys"][:, :]), writes=[B.dmy_buf["s"]])

    B.op("g", lambda g: g.dma_start(sb["eyebf"][:, :], eye_h[:, :]), writes=[beye], queue="qw")
    B.op("g", lambda g: g.dma_start(sb["wvT"][:, :, 0:512], wvt_h[:, :, 0:512]),
         writes=[bwv[0]], queue="qw")
    B.op("g", lambda g: g.dma_start(sb["wvT"][:, :, 512:1024], wvt_h[:, :, 512:1024]),
         writes=[bwv[1]], queue="qw")
    B.op("g", lambda g: g.dma_start(sb["woutT"][:, :, :], wout_h[:, :, :]), writes=[bwo], queue="qw")
    B.op("g", lambda g: g.dma_start(sb["gwt"][:, :, :], gwt_h[:, :, :]), writes=[bgwt], queue="qw")
    B.op("g", lambda g: g.dma_start(sb["a2t"][:, :, :], a2t_h[:, :, :]), writes=[ba2t], queue="qw")
    B.op("g", lambda g: g.dma_start(sb["b2"][:, :], b2_h[:, :]), writes=[bb2], queue="qw")
    for gidx in range(4):
        k0 = 2 * gidx
        B.op("g", lambda g, k0=k0: g.dma_start(sb["wiT"][:, k0:k0 + 2, :], wit_h[:, k0:k0 + 2, :]),
             writes=[bwiT[gidx]], queue="qw")
    for gidx in range(2):
        f0 = 8 * gidx
        B.op("g", lambda g, f0=f0: g.dma_start(sb["woTlo"][:, f0:f0 + 8, :], wot_h[:, f0:f0 + 8, :]),
             writes=[bwoT[gidx]], queue="qw")
    def load_x(src, c):
        cb = c % 2
        for j in range(JT):
            g = JT * c + j
            B.op("sy", lambda s, g=g, cb=cb, j=j: s.dma_start(sb["xsb%d" % cb][:, j, :], src[g]),
                 reads=([bx1s[g]] if src is x1s_h else []), writes=[xt[cb][j]], queue="qx")

    load_x(xh, 0)
    B.op("sy", lambda s: s.dma_start(sb["bv"][:, :], bv_h[:, :]), writes=[bbv], queue="qx")
    B.op("sy", lambda s: s.dma_start(sb["boutb"][:, :], boutb_h[:, :]), writes=[bboutb], queue="qx")
    B.op("sy", lambda s: s.dma_start(sb["gateb"][:, 0:1], gb_h[:, :]), writes=[bgb], queue="qx")
    B.op("sy", lambda s: s.dma_start(sb["wib"][:, :], wib_h[:, :]), writes=[bwib], queue="qx")

    def ln_batched(cb, inp_bufs, out_is_htmp=True):
        xs = sb["xsb%d" % cb]
        hm = sb["htmp%d" % cb]
        for j in range(JT):
            B.op("v", lambda v, j=j, xs=xs: v.reduce_sum(stc("s", j), xs[:, j, :], axis=AX.X),
                 reads=[inp_bufs[j]], writes=[bst["s"]])
        for j in range(JT):
            B.op("s", lambda s, j=j, xs=xs: s.activation(sb["trash"][:, :], xs[:, j, :], AF.Square,
                                                         bias=0.0, scale=1.0, accum_out=stc("q", j)),
                 reads=[inp_bufs[j]], writes=[btrash, bst["q"]])
        B.op("v", lambda v: v.tensor_scalar_mul(stc("m", 0, JT), stc("s", 0, JT), 1.0 / D),
             reads=[bst["s"]], writes=[bst["m"]])
        B.op("v", lambda v: v.tensor_scalar_mul(stc("e", 0, JT), stc("q", 0, JT), 1.0 / D),
             reads=[bst["q"]], writes=[bst["e"]])
        B.op("v", lambda v: v.tensor_mul(stc("ms", 0, JT), stc("m", 0, JT), stc("m", 0, JT)),
             reads=[bst["m"]], writes=[bst["ms"]])
        B.op("v", lambda v: v.scalar_tensor_tensor(stc("v", 0, JT), stc("e", 0, JT), EPS,
                                                   stc("ms", 0, JT), ALU.add, ALU.subtract),
             reads=[bst["e"], bst["ms"]], writes=[bst["v"]])
        B.op("s", lambda s: s.activation(stc("sd", 0, JT), stc("v", 0, JT), AF.Sqrt, bias=0.0, scale=1.0),
             reads=[bst["v"]], writes=[bst["sd"]])
        B.op("v", lambda v: v.reciprocal(stc("r", 0, JT), stc("sd", 0, JT)),
             reads=[bst["sd"]], writes=[bst["r"]])
        for j in range(JT):
            B.op("v", lambda v, j=j, xs=xs, hm=hm: v.tensor_scalar(
                hm[:, j, :], xs[:, j, :], stc("m", j), stc("r", j), ALU.subtract, ALU.mult),
                reads=[inp_bufs[j], bst["m"], bst["r"]], writes=[ht[cb][j]])

    def transposes(cb):
        hm = sb["htmp%d" % cb]
        for g2 in range(2):
            for dtl in range(4):
                dt = g2 * 4 + dtl
                for j in range(JT):
                    B.op("t", lambda t, g2=g2, dtl=dtl, dt=dt, j=j, hm=hm: t.transpose(
                        pT[g2][:, dtl * 256 + j * 128: dtl * 256 + (j + 1) * 128],
                        hm[:, j, dt * 128:(dt + 1) * 128], sb["eyebf"][:, :]),
                        reads=[ht[cb][j], beye], writes=[bpT[g2]])
            B.op("s", lambda s, g2=g2: s.copy(
                sb["hT"][:, g2 * 4:(g2 + 1) * 4, :],
                pT[g2][:, :].rearrange("p (a b) -> p a b", b=CH)),
                reads=[bpT[g2]], writes=[hTg[g2]])

    baoraw = [Buf() for _ in range(JT)]
    baotmp = Buf()

    def epi_quick(pOb, j_slot):
        B.op("s", lambda s, pOb=pOb, j_slot=j_slot: s.copy(sb["aoraw"][:, j_slot, :], pO[pOb][:, :]),
             reads=[bpO[pOb]], writes=[baoraw[j_slot]])

    def epi_rest(cb, j, bias_name, bias_buf, store_dst, g):
        xs = sb["xsb%d" % cb]
        ao = sb["aoraw"]
        B.op("v", lambda v, j=j, ao=ao: v.scalar_tensor_tensor(
            sb["aotmp"][:, :], ao[:, j, :], 1.0, sb[bias_name][:, :],
            ALU.mult, ALU.add, accum_out=stc("s2")),
            reads=[baoraw[j], bias_buf], writes=[bst["s2"], baotmp])
        B.op("s", lambda s: s.activation(sb["trash"][:, :], sb["aotmp"][:, :], AF.Square,
                                         bias=0.0, scale=1.0, accum_out=stc("q2")),
             reads=[baotmp], writes=[btrash, bst["q2"]])
        B.op("v", lambda v: v.tensor_scalar_mul(stc("m2"), stc("s2"), 1.0 / D),
             reads=[bst["s2"]], writes=[bst["m2"]])
        B.op("v", lambda v: v.tensor_scalar_mul(stc("e2"), stc("q2"), 1.0 / D),
             reads=[bst["q2"]], writes=[bst["e2"]])
        B.op("v", lambda v: v.tensor_mul(stc("ms2"), stc("m2"), stc("m2")),
             reads=[bst["m2"]], writes=[bst["ms2"]])
        B.op("v", lambda v: v.scalar_tensor_tensor(stc("v2"), stc("e2"), EPS, stc("ms2"),
                                                   ALU.add, ALU.subtract),
             reads=[bst["e2"], bst["ms2"]], writes=[bst["v2"]])
        B.op("s", lambda s: s.activation(stc("sd2"), stc("v2"), AF.Sqrt, bias=0.0, scale=1.0),
             reads=[bst["v2"]], writes=[bst["sd2"]])
        B.op("v", lambda v: v.reciprocal(stc("r2"), stc("sd2")), reads=[bst["sd2"]], writes=[bst["r2"]])
        B.op("v", lambda v: v.tensor_scalar(sb["aotmp"][:, :], sb["aotmp"][:, :], stc("m2"), stc("r2"),
                                            ALU.subtract, ALU.mult),
             reads=[baotmp, bst["m2"], bst["r2"]], writes=[baotmp])
        B.op("v", lambda v, j=j, xs=xs: v.tensor_add(xs[:, j, :], xs[:, j, :], sb["aotmp"][:, :]),
             reads=[xt[cb][j], baotmp], writes=[xt[cb][j]])
        extra = [bx1s[g]] if store_dst is x1s_h else []
        B.op("s", lambda s, g=g, j=j, xs=xs: s.dma_start(store_dst[g], xs[:, j, :]),
             reads=[xt[cb][j]], writes=extra, queue="qo")

    for c in range(NCH):
        cb = c % 2
        ln_batched(cb, xt[cb])
        transposes(cb)
        for dt in range(KT):
            pb = dt % 2
            for k in range(KT):
                B.op("t", lambda t, pb=pb, k=k, dt=dt: t.matmul(
                    pV[pb][:, 0:CH], sb["wvT"][:, k, dt * 128:(dt + 1) * 128], sb["hT"][:, k, :],
                    start=(k == 0), stop=(k == KT - 1)),
                    reads=[bwv[dt // 4], hTg[k // 4]], writes=[bpV[pb]])
            B.op("s", lambda s, pb=pb, dt=dt: s.activation(
                sb["vT"][:, dt, :], pV[pb][:, 0:CH], AF.Identity,
                bias=sb["bv"][:, dt:dt + 1], scale=1.0),
                reads=[bpV[pb], bbv], writes=[vTb[dt]])
        if c >= 1:
            for j in range(JT):
                epi_rest((c - 1) % 2, j, "boutb", bboutb, x1s_h, JT * (c - 1) + j)
        if c + 1 < NCH:
            load_x(xh, c + 1)
        for j in range(JT):
            for nch in range(2):
                for k in range(KT):
                    B.op("t", lambda t, j=j, nch=nch, k=k: t.matmul(
                        pO[j][:, nch * 512:(nch + 1) * 512],
                        sb["vT"][:, k, j * 128:(j + 1) * 128],
                        sb["woutT"][:, k, nch * 512:(nch + 1) * 512],
                        start=(k == 0), stop=(k == KT - 1)),
                        reads=[vTb[k], bwo], writes=[bpO[j]])
            epi_quick(j, j)
    for j in range(JT):
        epi_rest((NCH - 1) % 2, j, "boutb", bboutb, x1s_h, JT * (NCH - 1) + j)

    B.op("g", lambda g: g.dma_start(sb["woThi"][:, 0:8, :], wot_h[:, 16:24, :]),
         writes=[bwoT[2]] + bwv, queue="qw")
    B.op("g", lambda g: g.dma_start(sb["woThi"][:, 8:16, :], wot_h[:, 24:32, :]),
         writes=[bwoT[3], bwo], queue="qw")
    B.op("sy", lambda s: s.dma_start(sb["wobb"][:, :], wobb_h[:, :]),
         writes=[bwobb] + vTb, queue="qx")

    load_x(x1s_h, 0)
    for c in range(NCH):
        cb = c % 2
        ln_batched(cb, xt[cb])
        transposes(cb)
        for k in range(KT):
            B.op("t", lambda t, k=k: t.matmul(pV[0][0:8, 0:CH], sb["gwt"][:, k, :], sb["hT"][:, k, :],
                                              start=(k == 0), stop=(k == KT - 1)),
                 reads=[bgwt, hTg[k // 4]], writes=[bpV[0]])
        for k in range(KT):
            B.op("t", lambda t, k=k: t.matmul(pV[0][0:32, 256:256 + CH], sb["a2t"][:, k, :],
                                              sb["hT"][:, k, :], start=(k == 0), stop=(k == KT - 1)),
                 reads=[ba2t, hTg[k // 4]], writes=[bpV[0]])
        B.op("s", lambda s: s.activation(sb["lg"][:, :], pV[0][0:8, 0:CH], AF.Identity,
                                         bias=sb["gateb"][:, 0:1], scale=1.0),
             reads=[bpV[0], bgb], writes=[blg])
        for j in range(JT):
            B.op("t", lambda t, j=j: t.transpose(pT[1][:, j * 8:(j + 1) * 8],
                                                 sb["lg"][:, j * 128:(j + 1) * 128],
                                                 sb["eyebf"][0:8, 0:8]),
                 reads=[blg, beye], writes=[bpT[1]])
            B.op("s", lambda s, j=j: s.copy(sb["lgT"][:, j * 8:(j + 1) * 8], pT[1][:, j * 8:(j + 1) * 8]),
                 reads=[bpT[1]], writes=[blgT])
            B.op("v", lambda v, j=j: v.reduce_max(stc("mx", j), sb["lgT"][:, j * 8:(j + 1) * 8], axis=AX.X),
                 reads=[blgT], writes=[bst["mx"]])
            for r in range(4):
                B.op("v", lambda v, j=j, r=r: v.tensor_scalar(
                    sb["maskT"][:, j, r * 8:(r + 1) * 8], sb["lgT"][:, j * 8:(j + 1) * 8],
                    stc("mx", j), None, ALU.is_ge),
                    reads=[blgT, bst["mx"]], writes=[bmaskT[j]])
            B.op("t", lambda t, j=j: t.transpose(pT[1][0:32, 128 + j * 128:256 + j * 128],
                                                 sb["maskT"][:, j, :], sb["eyebf"][:, :]),
                 reads=[bmaskT[j], beye], writes=[bpT[1]])
            B.op("s", lambda s, j=j: s.copy(sb["mask32"][0:32, j * 128:(j + 1) * 128],
                                            pT[1][0:32, 128 + j * 128:256 + j * 128]),
                 reads=[bpT[1]], writes=[bmask32])
        B.op("v", lambda v: v.tensor_mul(sb["xam"][0:32, :], pV[0][0:32, 256:256 + CH],
                                         sb["mask32"][0:32, :]),
             reads=[bpV[0], bmask32], writes=[bxam])
        for ft in range(FT):
            if ft == 8:
                if c >= 1:
                    for j in range(JT):
                        epi_rest((c - 1) % 2, j, "wobb", bwobb, yh, JT * (c - 1) + j)
                if c + 1 < NCH:
                    load_x(x1s_h, c + 1)
            ib = ft % 2
            for k in range(KT):
                B.op("t", lambda t, k=k, ft=ft: t.matmul(
                    pV[1][:, 0:CH], sb["wiT"][:, k, ft * 128:(ft + 1) * 128], sb["hT"][:, k, :],
                    start=(k == 0), stop=False),
                    reads=[bwiT[k // 2], hTg[k // 4]], writes=[bpV[1]])
            B.op("t", lambda t, ft=ft: t.matmul(pV[1][:, 0:CH], sb["b2"][0:32, ft * 128:(ft + 1) * 128],
                                                sb["xam"][0:32, :], start=False, stop=True),
                 reads=[bb2, bxam], writes=[bpV[1]])
            B.op("s", lambda s, ib=ib, ft=ft: s.activation(
                sb["if%d" % ib][:, :], pV[1][:, 0:CH], AF.Relu,
                bias=sb["wib"][:, ft:ft + 1], scale=1.0),
                reads=[bpV[1], bwib], writes=[bif[ib]])
            wo_sb = sb["woTlo"] if ft < 16 else sb["woThi"]
            fl = ft if ft < 16 else ft - 16
            for j in range(JT):
                for nch in range(2):
                    B.op("t", lambda t, ib=ib, j=j, nch=nch, wo_sb=wo_sb, fl=fl, ft=ft: t.matmul(
                        pO[j][:, nch * 512:(nch + 1) * 512],
                        sb["if%d" % ib][:, j * 128:(j + 1) * 128],
                        wo_sb[:, fl, nch * 512:(nch + 1) * 512],
                        start=(ft == 0), stop=(ft == FT - 1)),
                        reads=[bif[ib], bwoT[ft // 8]], writes=[bpO[j]])
        for j in range(JT):
            epi_quick(j, j)
    for j in range(JT):
        epi_rest((NCH - 1) % 2, j, "wobb", bwobb, yh, JT * (NCH - 1) + j)

    B.raw_wait("s", "qo", 16 * B.n["qo"])

    with (
        nc.semaphore("sem_t") as sem_t,
        nc.semaphore("sem_v") as sem_v,
        nc.semaphore("sem_s") as sem_s,
        nc.semaphore("sem_qx") as sem_qx,
        nc.semaphore("sem_qw") as sem_qw,
        nc.semaphore("sem_qo") as sem_qo,
        nc.Block() as block,
    ):
        sems = {"t": sem_t, "v": sem_v, "s": sem_s, "qx": sem_qx, "qw": sem_qw, "qo": sem_qo}

        def replay(eng_key):
            def run(h):
                for item in B.ops[eng_key]:
                    if item[0] == "wait":
                        h.wait_ge(sems[item[1]], item[2])
                    else:
                        _, fn, semkey, inc = item
                        fn(h).then_inc(sems[semkey], inc)
            return run

        @block.tensor
        def _(t):
            replay("t")(t)

        @block.vector
        def _(v):
            replay("v")(v)

        @block.scalar
        def _(s):
            replay("s")(s)

        @block.sync
        def _(sy):
            replay("sy")(sy)

        @block.gpsimd
        def _(g):
            replay("g")(g)

    return nc


def _prep(inp):
    w_in = inp["L0_attn_w_in"]
    b_in = inp["L0_attn_b_in"]
    w_out = inp["L0_attn_w_out"]
    b_out = inp["L0_attn_b_out"]
    gate_W = inp["L0_ffn_gate_W"]
    gate_b = inp["L0_ffn_gate_b"]
    wi_W = inp["L0_ffn_wi_W"]
    wi_b = inp["L0_ffn_wi_b"]
    wo_W = inp["L0_ffn_wo_W"]
    wo_b = inp["L0_ffn_wo_b"]
    lora_A = inp["L0_ffn_lora_A"]
    lora_B = inp["L0_ffn_lora_B"]

    A = np.ascontiguousarray
    wv = w_in.reshape(16, 3, 64, D)[:, 2].reshape(D, D)
    d = {
        "wvt": A(wv.T.reshape(KT, 128, D).transpose(1, 0, 2)),
        "wout": A(w_out.T.reshape(KT, 128, D).transpose(1, 0, 2)),
        "bvp": A(b_in.reshape(16, 3, 64)[:, 2].reshape(KT, 128).T),
        "boutb": A(np.broadcast_to(b_out, (128, D))),
        "gwt": A(gate_W.T.reshape(KT, 128, 8).transpose(1, 0, 2)),
        "gb": A(gate_b.reshape(8, 1)),
        "wit": A(wi_W.T.reshape(KT, 128, FF).transpose(1, 0, 2)),
        "wibp": A(wi_b.reshape(FT, 128).T),
        "wot": A(wo_W.T.reshape(FT, 128, D).transpose(1, 0, 2)),
        "wobb": A(np.broadcast_to(wo_b, (128, D))),
        "a2t": A(lora_A.transpose(1, 0, 2).reshape(32, D).T.reshape(KT, 128, 32).transpose(1, 0, 2)),
        "b2": A(lora_B.transpose(2, 0, 1).reshape(32, FF)),
        "eye": np.eye(128, dtype=np.float32),
    }
    return {k: np.asarray(v, dtype=np.float32) for k, v in d.items()}


def kernel(**inputs):
    inputs = {k: np.asarray(v) for k, v in inputs.items()}
    x = np.ascontiguousarray(inputs["L0_attn_x"], dtype=np.float32)
    xf = x.reshape(8 * 2048, D)
    w = _prep(inputs)
    nc = build_nc()
    in_maps = []
    for c in range(8):
        m = dict(w)
        m["x"] = np.ascontiguousarray(xf[c * TPC:(c + 1) * TPC].reshape(NT, 128, D))
        in_maps.append(m)
    res = run_bass_kernel_spmd(nc, in_maps, core_ids=list(range(8)))
    out = np.concatenate([r["y"].reshape(TPC, D) for r in res.results], axis=0)
    return out.reshape(8, 2048, D).astype(np.float32)


if __name__ == "__main__":
    rng = np.random.default_rng(0)
    fake = {
        "L0_attn_x": rng.standard_normal((8, 2048, 1024), dtype=np.float32),
        "L0_attn_w_in": rng.standard_normal((3072, 1024), dtype=np.float32) * 0.02,
        "L0_attn_b_in": rng.standard_normal((3072,), dtype=np.float32) * 0.02,
        "L0_attn_w_out": rng.standard_normal((1024, 1024), dtype=np.float32) * 0.02,
        "L0_attn_b_out": rng.standard_normal((1024,), dtype=np.float32) * 0.02,
        "L0_ffn_gate_W": rng.standard_normal((8, 1024), dtype=np.float32) * 0.03,
        "L0_ffn_gate_b": rng.standard_normal((8,), dtype=np.float32) * 0.01,
        "L0_ffn_wi_W": rng.standard_normal((4096, 1024), dtype=np.float32) * 0.03,
        "L0_ffn_wi_b": rng.standard_normal((4096,), dtype=np.float32) * 0.01,
        "L0_ffn_wo_W": rng.standard_normal((1024, 4096), dtype=np.float32) * 0.015,
        "L0_ffn_wo_b": rng.standard_normal((1024,), dtype=np.float32) * 0.01,
        "L0_ffn_lora_A": rng.standard_normal((8, 4, 1024), dtype=np.float32) * 0.03,
        "L0_ffn_lora_B": rng.standard_normal((8, 4096, 4), dtype=np.float32) * 0.02,
    }
    out = kernel(**fake)
    print("out", out.shape, out.dtype, float(np.abs(out).mean()))


# version 07
# speedup vs baseline: 1.1930x; 1.1930x over previous
import numpy as np
import concourse.bass as bass
import concourse.mybir as mybir
from concourse.bass_utils import run_bass_kernel_spmd

F32 = mybir.dt.float32
BF16 = mybir.dt.bfloat16
AF = mybir.ActivationFunctionType
ALU = mybir.AluOpType
AX = mybir.AxisListType

D = 1024
FF = 4096
KT = 8
FT = 32
TPC = 2048
CH = 256
NCH = TPC // CH
JT = CH // 128
NT = TPC // 128
EPS = 1e-5


class Buf:
    __slots__ = ("w", "r")

    def __init__(self):
        self.w = None
        self.r = []


class Bld:

    ENG_SECT = {"t": "tensor", "v": "vector", "s": "scalar", "sy": "sync", "g": "gpsimd"}

    def __init__(self):
        self.ops = {e: [] for e in self.ENG_SECT}
        self.n = {k: 0 for k in ("t", "v", "s", "qx", "qw", "qo")}
        self.waited = {}
        self.dummy_fn = {}
        self.dmy_buf = {"v": Buf(), "s": Buf()}

    def _wait(self, eng, dep):
        stream, val = dep
        if stream == eng:
            if eng == "t":
                return
        key = (eng, stream)
        if self.waited.get(key, 0) >= val:
            return
        self.waited[key] = val
        self.ops[eng].append(("wait", stream, val))

    def op(self, eng, fn, reads=(), writes=(), queue=None):
        if eng in ("v", "s") and self.dummy_fn.get(eng) is not None:
            last = (eng, self.n[eng])
            if any(b.w == last for b in reads):
                self.op(eng, self.dummy_fn[eng], reads=(), writes=(self.dmy_buf[eng],))
        for b in reads:
            if b.w is not None:
                self._wait(eng, b.w)
        for b in writes:
            if b.w is not None:
                self._wait(eng, b.w)
            for rd in b.r:
                self._wait(eng, rd)
        if queue is None:
            self.n[eng] += 1
            tok = (eng, self.n[eng])
            self.ops[eng].append(("do", fn, eng, 1))
        else:
            if self.n[queue] > 0:
                self._wait(eng, (queue, 16 * self.n[queue]))
            self.n[queue] += 1
            tok = (queue, 16 * self.n[queue])
            self.ops[eng].append(("do", fn, queue, 16))
        for b in reads:
            b.r.append(tok)
        for b in writes:
            b.w = tok
            b.r = []

    def raw_wait(self, eng, stream, val):
        key = (eng, stream)
        if self.waited.get(key, 0) >= val:
            return
        self.waited[key] = val
        self.ops[eng].append(("wait", stream, val))


def build_nc():
    nc = bass.Bass("TRN2", target_bir_lowering=False)

    xh = nc.dram_tensor("x", [NT, 128, D], F32, kind="ExternalInput")
    wvt_h = nc.dram_tensor("wvt", [128, KT, D], F32, kind="ExternalInput")
    wout_h = nc.dram_tensor("wout", [128, KT, D], F32, kind="ExternalInput")
    bv_h = nc.dram_tensor("bvp", [128, KT], F32, kind="ExternalInput")
    boutb_h = nc.dram_tensor("boutb", [128, D], F32, kind="ExternalInput")
    gwt_h = nc.dram_tensor("gwt", [128, KT, 8], F32, kind="ExternalInput")
    gb_h = nc.dram_tensor("gb", [8, 1], F32, kind="ExternalInput")
    wit_h = nc.dram_tensor("wit", [128, KT, FF], F32, kind="ExternalInput")
    wib_h = nc.dram_tensor("wibp", [128, FT], F32, kind="ExternalInput")
    wot_h = nc.dram_tensor("wot", [128, FT, D], F32, kind="ExternalInput")
    wobb_h = nc.dram_tensor("wobb", [128, D], F32, kind="ExternalInput")
    a2t_h = nc.dram_tensor("a2t", [128, KT, 32], F32, kind="ExternalInput")
    b2_h = nc.dram_tensor("b2", [32, FF], F32, kind="ExternalInput")
    eye_h = nc.dram_tensor("eye", [128, 128], F32, kind="ExternalInput")
    x1s_h = nc.dram_tensor("x1s", [NT, 128, D], F32, kind="Internal")
    yh = nc.dram_tensor("y", [NT, 128, D], F32, kind="ExternalOutput")

    layout = []

    def L(name, shape, dtype):
        layout.append((name, shape, dtype))
        return name

    static = [
        ("xsb0", [128, JT, D], F32), ("xsb1", [128, JT, D], F32),
        ("htmp0", [128, JT, D], BF16), ("htmp1", [128, JT, D], BF16),
        ("hT", [128, KT, CH], BF16),
        ("aotmp", [128, D], F32),
        ("trash", [128, D], F32),
        ("eyebf", [128, 128], BF16),
        ("stats", [128, 64], F32),
        ("lg", [8, CH], BF16),
        ("lgT", [128, 16], BF16),
        ("maskT", [128, JT, 32], BF16),
        ("mask32", [32, CH], F32),
        ("xam", [32, CH], BF16),
        ("bv", [128, KT], F32),
        ("gateb", [8, 16], F32),
        ("wib", [128, FT], F32),
        ("boutb", [128, D], F32),
        ("a2t", [128, KT, 32], BF16),
        ("b2", [32, FF], BF16),
        ("gwt", [128, KT, 8], BF16),
        ("if0", [128, CH], BF16), ("if1", [128, CH], BF16),
        ("dmyv", [128, 64], F32), ("dmys", [128, 64], F32),
        ("aoraw", [128, JT, D], F32),
        ("wiT", [128, KT, FF], BF16),
        ("woTlo", [128, 16, D], BF16),
    ]
    dt_size = {F32: 4, BF16: 2}
    off = 0
    offs = {}
    for name, shape, dtype in static:
        per_part = int(np.prod(shape[1:])) * dt_size[dtype]
        offs[name] = off
        off += (per_part + 31) // 32 * 32
    alias_base = off
    alias = [
        ("wvT", [128, KT, D], BF16, 0),
        ("woutT", [128, KT, D], BF16, 16384),
        ("vT", [128, KT, CH], BF16, 32768),
        ("woThi", [128, 16, D], BF16, 0),
        ("wobb", [128, D], F32, 32768),
    ]
    total = alias_base + 40960
    arena = nc.alloc_sbuf_tensor("arena", [128, total], mybir.dt.uint8)
    base = nc.lookup_mloc(arena).addr

    sb = {}
    for name, shape, dtype in static:
        sb[name] = nc.alloc_sbuf_tensor_at(name, shape, dtype, offset=base + offs[name])
    for name, shape, dtype, aoff in alias:
        sb[name] = nc.alloc_sbuf_tensor_at(name, shape, dtype, offset=base + alias_base + aoff)

    pT0 = nc.alloc_psum_tensor("pT0", [128, 1024], BF16)
    pV0 = nc.alloc_psum_tensor("pV0", [128, 512], F32)
    pV1 = nc.alloc_psum_tensor("pV1", [128, 512], F32)
    pW2 = nc.alloc_psum_tensor("pW2", [128, 512], F32)
    pO0 = nc.alloc_psum_tensor("pO0", [128, 1024], F32)
    pO1 = nc.alloc_psum_tensor("pO1", [128, 1024], F32)
    pT = [pT0, pT0]
    pV = [pV0, pV1]
    pWI = [pV1, pW2]
    pO = [pO0, pO1]

    B = Bld()
    B.dummy_fn["v"] = lambda v: v.tensor_copy(sb["dmyv"][:, 0:32], sb["dmyv"][:, 32:64])
    B.dummy_fn["s"] = lambda s: s.copy(sb["dmys"][:, 0:32], sb["dmys"][:, 32:64])
    xt = [[Buf() for _ in range(JT)] for _ in range(2)]
    ht = [[Buf() for _ in range(JT)] for _ in range(2)]
    hTg = [Buf(), Buf()]
    vTb = [Buf() for _ in range(KT)]
    _bpT0 = Buf()
    bpT = [_bpT0, _bpT0]
    bpV = [Buf(), Buf()]
    bpWI = [bpV[1], Buf()]
    bpO = [Buf(), Buf()]
    bwv = [Buf(), Buf()]; bwo = Buf(); bgwt = Buf(); ba2t = Buf(); bb2 = Buf(); beye = Buf()
    bwiT = [Buf() for _ in range(4)]
    bwoT = [Buf() for _ in range(4)]
    bbv = Buf(); bboutb = Buf(); bgb = Buf(); bwib = Buf(); bwobb = Buf()
    bst = {k: Buf() for k in ("s", "q", "m", "e", "ms", "v", "sd", "r",
                              "s2", "q2", "m2", "e2", "ms2", "v2", "sd2", "r2", "mx")}
    blg = Buf(); blgT = Buf(); bmaskT = [Buf() for _ in range(JT)]
    bmask32 = Buf(); bxam = Buf()
    bif = [Buf(), Buf()]
    bx1s = [Buf() for _ in range(NT)]
    btrash = Buf()

    SC = {"s": 0, "q": 2, "m": 4, "e": 6, "ms": 8, "v": 10, "sd": 12, "r": 14,
          "s2": 16, "q2": 17, "m2": 18, "e2": 19, "ms2": 20, "v2": 21, "sd2": 22,
          "r2": 23, "mx": 24}

    st = sb["stats"]

    def stc(key, j=0, n=1):
        c = SC[key] + j
        return st[:, c:c + n]

    B.op("v", lambda v: v.memset(sb["dmyv"][:, :], 0.0), writes=[B.dmy_buf["v"]])
    B.op("s", lambda s: s.memzero(sb["dmys"][:, :]), writes=[B.dmy_buf["s"]])

    B.op("g", lambda g: g.dma_start(sb["eyebf"][:, :], eye_h[:, :]), writes=[beye], queue="qw")
    B.op("g", lambda g: g.dma_start(sb["wvT"][:, :, 0:512], wvt_h[:, :, 0:512]),
         writes=[bwv[0]], queue="qw")
    B.op("g", lambda g: g.dma_start(sb["wvT"][:, :, 512:1024], wvt_h[:, :, 512:1024]),
         writes=[bwv[1]], queue="qw")
    B.op("g", lambda g: g.dma_start(sb["woutT"][:, :, :], wout_h[:, :, :]), writes=[bwo], queue="qw")
    B.op("g", lambda g: g.dma_start(sb["gwt"][:, :, :], gwt_h[:, :, :]), writes=[bgwt], queue="qw")
    B.op("g", lambda g: g.dma_start(sb["a2t"][:, :, :], a2t_h[:, :, :]), writes=[ba2t], queue="qw")
    B.op("g", lambda g: g.dma_start(sb["b2"][:, :], b2_h[:, :]), writes=[bb2], queue="qw")
    for gidx in range(4):
        k0 = 2 * gidx
        B.op("g", lambda g, k0=k0: g.dma_start(sb["wiT"][:, k0:k0 + 2, :], wit_h[:, k0:k0 + 2, :]),
             writes=[bwiT[gidx]], queue="qw")
    for gidx in range(2):
        f0 = 8 * gidx
        B.op("g", lambda g, f0=f0: g.dma_start(sb["woTlo"][:, f0:f0 + 8, :], wot_h[:, f0:f0 + 8, :]),
             writes=[bwoT[gidx]], queue="qw")
    def load_x(src, c):
        cb = c % 2
        for j in range(JT):
            g = JT * c + j
            B.op("sy", lambda s, g=g, cb=cb, j=j: s.dma_start(sb["xsb%d" % cb][:, j, :], src[g]),
                 reads=([bx1s[g]] if src is x1s_h else []), writes=[xt[cb][j]], queue="qx")

    load_x(xh, 0)
    B.op("sy", lambda s: s.dma_start(sb["bv"][:, :], bv_h[:, :]), writes=[bbv], queue="qx")
    B.op("sy", lambda s: s.dma_start(sb["boutb"][:, :], boutb_h[:, :]), writes=[bboutb], queue="qx")
    B.op("sy", lambda s: s.dma_start(sb["gateb"][:, 0:1], gb_h[:, :]), writes=[bgb], queue="qx")
    B.op("sy", lambda s: s.dma_start(sb["wib"][:, :], wib_h[:, :]), writes=[bwib], queue="qx")

    def ln_batched(cb, inp_bufs, out_is_htmp=True):
        xs = sb["xsb%d" % cb]
        hm = sb["htmp%d" % cb]
        for j in range(JT):
            B.op("v", lambda v, j=j, xs=xs: v.reduce_sum(stc("s", j), xs[:, j, :], axis=AX.X),
                 reads=[inp_bufs[j]], writes=[bst["s"]])
        for j in range(JT):
            B.op("s", lambda s, j=j, xs=xs: s.activation(sb["trash"][:, :], xs[:, j, :], AF.Square,
                                                         bias=0.0, scale=1.0, accum_out=stc("q", j)),
                 reads=[inp_bufs[j]], writes=[btrash, bst["q"]])
        B.op("v", lambda v: v.tensor_scalar_mul(stc("m", 0, JT), stc("s", 0, JT), 1.0 / D),
             reads=[bst["s"]], writes=[bst["m"]])
        B.op("v", lambda v: v.tensor_scalar_mul(stc("e", 0, JT), stc("q", 0, JT), 1.0 / D),
             reads=[bst["q"]], writes=[bst["e"]])
        B.op("v", lambda v: v.tensor_mul(stc("ms", 0, JT), stc("m", 0, JT), stc("m", 0, JT)),
             reads=[bst["m"]], writes=[bst["ms"]])
        B.op("v", lambda v: v.scalar_tensor_tensor(stc("v", 0, JT), stc("e", 0, JT), EPS,
                                                   stc("ms", 0, JT), ALU.add, ALU.subtract),
             reads=[bst["e"], bst["ms"]], writes=[bst["v"]])
        B.op("s", lambda s: s.activation(stc("sd", 0, JT), stc("v", 0, JT), AF.Sqrt, bias=0.0, scale=1.0),
             reads=[bst["v"]], writes=[bst["sd"]])
        B.op("v", lambda v: v.reciprocal(stc("r", 0, JT), stc("sd", 0, JT)),
             reads=[bst["sd"]], writes=[bst["r"]])
        for j in range(JT):
            B.op("v", lambda v, j=j, xs=xs, hm=hm: v.tensor_scalar(
                hm[:, j, :], xs[:, j, :], stc("m", j), stc("r", j), ALU.subtract, ALU.mult),
                reads=[inp_bufs[j], bst["m"], bst["r"]], writes=[ht[cb][j]])

    def transposes(cb):
        hm = sb["htmp%d" % cb]
        for g2 in range(2):
            for dtl in range(4):
                dt = g2 * 4 + dtl
                for j in range(JT):
                    B.op("t", lambda t, g2=g2, dtl=dtl, dt=dt, j=j, hm=hm: t.transpose(
                        pT[g2][:, dtl * 256 + j * 128: dtl * 256 + (j + 1) * 128],
                        hm[:, j, dt * 128:(dt + 1) * 128], sb["eyebf"][:, :]),
                        reads=[ht[cb][j], beye], writes=[bpT[g2]])
            B.op("s", lambda s, g2=g2: s.copy(
                sb["hT"][:, g2 * 4:(g2 + 1) * 4, :],
                pT[g2][:, :].rearrange("p (a b) -> p a b", b=CH)),
                reads=[bpT[g2]], writes=[hTg[g2]])

    baoraw = [Buf() for _ in range(JT)]
    baotmp = Buf()

    def epi_quick(pOb, j_slot):
        B.op("s", lambda s, pOb=pOb, j_slot=j_slot: s.copy(sb["aoraw"][:, j_slot, :], pO[pOb][:, :]),
             reads=[bpO[pOb]], writes=[baoraw[j_slot]])

    def epi_rest(cb, j, bias_name, bias_buf, store_dst, g):
        xs = sb["xsb%d" % cb]
        ao = sb["aoraw"]
        B.op("v", lambda v, j=j, ao=ao: v.scalar_tensor_tensor(
            sb["aotmp"][:, :], ao[:, j, :], 1.0, sb[bias_name][:, :],
            ALU.mult, ALU.add, accum_out=stc("s2")),
            reads=[baoraw[j], bias_buf], writes=[bst["s2"], baotmp])
        B.op("v", lambda v: v.scalar_tensor_tensor(sb["trash"][:, :], sb["aotmp"][:, :], 1.0,
                                                   sb["aotmp"][:, :], ALU.mult, ALU.mult,
                                                   accum_out=stc("q2")),
             reads=[baotmp], writes=[btrash, bst["q2"]])
        B.op("v", lambda v: v.tensor_scalar_mul(stc("m2"), stc("s2"), 1.0 / D),
             reads=[bst["s2"]], writes=[bst["m2"]])
        B.op("v", lambda v: v.tensor_scalar_mul(stc("e2"), stc("q2"), 1.0 / D),
             reads=[bst["q2"]], writes=[bst["e2"]])
        B.op("v", lambda v: v.tensor_mul(stc("ms2"), stc("m2"), stc("m2")),
             reads=[bst["m2"]], writes=[bst["ms2"]])
        B.op("v", lambda v: v.scalar_tensor_tensor(stc("v2"), stc("e2"), EPS, stc("ms2"),
                                                   ALU.add, ALU.subtract),
             reads=[bst["e2"], bst["ms2"]], writes=[bst["v2"]])
        B.op("s", lambda s: s.activation(stc("sd2"), stc("v2"), AF.Sqrt, bias=0.0, scale=1.0),
             reads=[bst["v2"]], writes=[bst["sd2"]])
        B.op("v", lambda v: v.reciprocal(stc("r2"), stc("sd2")), reads=[bst["sd2"]], writes=[bst["r2"]])
        B.op("v", lambda v: v.tensor_scalar(sb["aotmp"][:, :], sb["aotmp"][:, :], stc("m2"), stc("r2"),
                                            ALU.subtract, ALU.mult),
             reads=[baotmp, bst["m2"], bst["r2"]], writes=[baotmp])
        B.op("v", lambda v, j=j, xs=xs: v.tensor_add(xs[:, j, :], xs[:, j, :], sb["aotmp"][:, :]),
             reads=[xt[cb][j], baotmp], writes=[xt[cb][j]])
        extra = [bx1s[g]] if store_dst is x1s_h else []
        B.op("sy", lambda s, g=g, j=j, xs=xs: s.dma_start(store_dst[g], xs[:, j, :]),
             reads=[xt[cb][j]], writes=extra, queue="qx")

    for c in range(NCH):
        cb = c % 2
        ln_batched(cb, xt[cb])
        transposes(cb)
        for dt in range(KT):
            pb = dt % 2
            for k in range(KT):
                B.op("t", lambda t, pb=pb, k=k, dt=dt: t.matmul(
                    pV[pb][:, 0:CH], sb["wvT"][:, k, dt * 128:(dt + 1) * 128], sb["hT"][:, k, :],
                    start=(k == 0), stop=(k == KT - 1)),
                    reads=[bwv[dt // 4], hTg[k // 4]], writes=[bpV[pb]])
            B.op("s", lambda s, pb=pb, dt=dt: s.activation(
                sb["vT"][:, dt, :], pV[pb][:, 0:CH], AF.Identity,
                bias=sb["bv"][:, dt:dt + 1], scale=1.0),
                reads=[bpV[pb], bbv], writes=[vTb[dt]])
        if c >= 1:
            for j in range(JT):
                epi_rest((c - 1) % 2, j, "boutb", bboutb, x1s_h, JT * (c - 1) + j)
        if c + 1 < NCH:
            load_x(xh, c + 1)
        for j in range(JT):
            for nch in range(2):
                for k in range(KT):
                    B.op("t", lambda t, j=j, nch=nch, k=k: t.matmul(
                        pO[j][:, nch * 512:(nch + 1) * 512],
                        sb["vT"][:, k, j * 128:(j + 1) * 128],
                        sb["woutT"][:, k, nch * 512:(nch + 1) * 512],
                        start=(k == 0), stop=(k == KT - 1)),
                        reads=[vTb[k], bwo], writes=[bpO[j]])
            epi_quick(j, j)
    for j in range(JT):
        epi_rest((NCH - 1) % 2, j, "boutb", bboutb, x1s_h, JT * (NCH - 1) + j)

    B.op("g", lambda g: g.dma_start(sb["woThi"][:, 0:8, :], wot_h[:, 16:24, :]),
         writes=[bwoT[2]] + bwv, queue="qw")
    B.op("g", lambda g: g.dma_start(sb["woThi"][:, 8:16, :], wot_h[:, 24:32, :]),
         writes=[bwoT[3], bwo], queue="qw")
    B.op("sy", lambda s: s.dma_start(sb["wobb"][:, :], wobb_h[:, :]),
         writes=[bwobb] + vTb, queue="qx")

    load_x(x1s_h, 0)
    for c in range(NCH):
        cb = c % 2
        ln_batched(cb, xt[cb])
        transposes(cb)
        for k in range(KT):
            B.op("t", lambda t, k=k: t.matmul(pV[0][0:8, 0:CH], sb["gwt"][:, k, :], sb["hT"][:, k, :],
                                              start=(k == 0), stop=(k == KT - 1)),
                 reads=[bgwt, hTg[k // 4]], writes=[bpV[0]])
        for k in range(KT):
            B.op("t", lambda t, k=k: t.matmul(pV[0][0:32, 256:256 + CH], sb["a2t"][:, k, :],
                                              sb["hT"][:, k, :], start=(k == 0), stop=(k == KT - 1)),
                 reads=[ba2t, hTg[k // 4]], writes=[bpV[0]])
        B.op("s", lambda s: s.activation(sb["lg"][:, :], pV[0][0:8, 0:CH], AF.Identity,
                                         bias=sb["gateb"][:, 0:1], scale=1.0),
             reads=[bpV[0], bgb], writes=[blg])
        for j in range(JT):
            B.op("t", lambda t, j=j: t.transpose(pT[1][:, j * 8:(j + 1) * 8],
                                                 sb["lg"][:, j * 128:(j + 1) * 128],
                                                 sb["eyebf"][0:8, 0:8]),
                 reads=[blg, beye], writes=[bpT[1]])
            B.op("s", lambda s, j=j: s.copy(sb["lgT"][:, j * 8:(j + 1) * 8], pT[1][:, j * 8:(j + 1) * 8]),
                 reads=[bpT[1]], writes=[blgT])
            B.op("v", lambda v, j=j: v.reduce_max(stc("mx", j), sb["lgT"][:, j * 8:(j + 1) * 8], axis=AX.X),
                 reads=[blgT], writes=[bst["mx"]])
            for r in range(4):
                B.op("v", lambda v, j=j, r=r: v.tensor_scalar(
                    sb["maskT"][:, j, r * 8:(r + 1) * 8], sb["lgT"][:, j * 8:(j + 1) * 8],
                    stc("mx", j), None, ALU.is_ge),
                    reads=[blgT, bst["mx"]], writes=[bmaskT[j]])
            B.op("t", lambda t, j=j: t.transpose(pT[1][0:32, 128 + j * 128:256 + j * 128],
                                                 sb["maskT"][:, j, :], sb["eyebf"][:, :]),
                 reads=[bmaskT[j], beye], writes=[bpT[1]])
            B.op("s", lambda s, j=j: s.copy(sb["mask32"][0:32, j * 128:(j + 1) * 128],
                                            pT[1][0:32, 128 + j * 128:256 + j * 128]),
                 reads=[bpT[1]], writes=[bmask32])
        B.op("v", lambda v: v.tensor_mul(sb["xam"][0:32, :], pV[0][0:32, 256:256 + CH],
                                         sb["mask32"][0:32, :]),
             reads=[bpV[0], bmask32], writes=[bxam])
        def wo_emit(ft):
            ib = ft % 2
            wo_sb = sb["woTlo"] if ft < 16 else sb["woThi"]
            fl = ft if ft < 16 else ft - 16
            for j in range(JT):
                for nch in range(2):
                    B.op("t", lambda t, ib=ib, j=j, nch=nch, wo_sb=wo_sb, fl=fl, ft=ft: t.matmul(
                        pO[j][:, nch * 512:(nch + 1) * 512],
                        sb["if%d" % ib][:, j * 128:(j + 1) * 128],
                        wo_sb[:, fl, nch * 512:(nch + 1) * 512],
                        start=(ft == 0), stop=(ft == FT - 1)),
                        reads=[bif[ib], bwoT[ft // 8]], writes=[bpO[j]])

        for ft in range(FT):
            if ft == 8:
                if c >= 1:
                    for j in range(JT):
                        epi_rest((c - 1) % 2, j, "wobb", bwobb, yh, JT * (c - 1) + j)
                if c + 1 < NCH:
                    load_x(x1s_h, c + 1)
            ib = ft % 2
            pw = pWI[ib]
            bpw = bpWI[ib]
            for k in range(KT):
                B.op("t", lambda t, k=k, ft=ft, pw=pw: t.matmul(
                    pw[:, 0:CH], sb["wiT"][:, k, ft * 128:(ft + 1) * 128], sb["hT"][:, k, :],
                    start=(k == 0), stop=False),
                    reads=[bwiT[k // 2], hTg[k // 4]], writes=[bpw])
            B.op("t", lambda t, ft=ft, pw=pw: t.matmul(pw[:, 0:CH], sb["b2"][0:32, ft * 128:(ft + 1) * 128],
                                                       sb["xam"][0:32, :], start=False, stop=True),
                 reads=[bb2, bxam], writes=[bpw])
            B.op("s", lambda s, ib=ib, ft=ft, pw=pw: s.activation(
                sb["if%d" % ib][:, :], pw[:, 0:CH], AF.Relu,
                bias=sb["wib"][:, ft:ft + 1], scale=1.0),
                reads=[bpw, bwib], writes=[bif[ib]])
            if ft >= 1:
                wo_emit(ft - 1)
        wo_emit(FT - 1)
        for j in range(JT):
            epi_quick(j, j)
    for j in range(JT):
        epi_rest((NCH - 1) % 2, j, "wobb", bwobb, yh, JT * (NCH - 1) + j)

    B.raw_wait("sy", "qx", 16 * B.n["qx"])

    with (
        nc.semaphore("sem_t") as sem_t,
        nc.semaphore("sem_v") as sem_v,
        nc.semaphore("sem_s") as sem_s,
        nc.semaphore("sem_qx") as sem_qx,
        nc.semaphore("sem_qw") as sem_qw,
        nc.semaphore("sem_qo") as sem_qo,
        nc.Block() as block,
    ):
        sems = {"t": sem_t, "v": sem_v, "s": sem_s, "qx": sem_qx, "qw": sem_qw, "qo": sem_qo}

        def replay(eng_key):
            def run(h):
                for item in B.ops[eng_key]:
                    if item[0] == "wait":
                        h.wait_ge(sems[item[1]], item[2])
                    else:
                        _, fn, semkey, inc = item
                        fn(h).then_inc(sems[semkey], inc)
            return run

        @block.tensor
        def _(t):
            replay("t")(t)

        @block.vector
        def _(v):
            replay("v")(v)

        @block.scalar
        def _(s):
            replay("s")(s)

        @block.sync
        def _(sy):
            replay("sy")(sy)

        @block.gpsimd
        def _(g):
            replay("g")(g)

    return nc


def _prep(inp):
    w_in = inp["L0_attn_w_in"]
    b_in = inp["L0_attn_b_in"]
    w_out = inp["L0_attn_w_out"]
    b_out = inp["L0_attn_b_out"]
    gate_W = inp["L0_ffn_gate_W"]
    gate_b = inp["L0_ffn_gate_b"]
    wi_W = inp["L0_ffn_wi_W"]
    wi_b = inp["L0_ffn_wi_b"]
    wo_W = inp["L0_ffn_wo_W"]
    wo_b = inp["L0_ffn_wo_b"]
    lora_A = inp["L0_ffn_lora_A"]
    lora_B = inp["L0_ffn_lora_B"]

    A = np.ascontiguousarray
    wv = w_in.reshape(16, 3, 64, D)[:, 2].reshape(D, D)
    d = {
        "wvt": A(wv.T.reshape(KT, 128, D).transpose(1, 0, 2)),
        "wout": A(w_out.T.reshape(KT, 128, D).transpose(1, 0, 2)),
        "bvp": A(b_in.reshape(16, 3, 64)[:, 2].reshape(KT, 128).T),
        "boutb": A(np.broadcast_to(b_out, (128, D))),
        "gwt": A(gate_W.T.reshape(KT, 128, 8).transpose(1, 0, 2)),
        "gb": A(gate_b.reshape(8, 1)),
        "wit": A(wi_W.T.reshape(KT, 128, FF).transpose(1, 0, 2)),
        "wibp": A(wi_b.reshape(FT, 128).T),
        "wot": A(wo_W.T.reshape(FT, 128, D).transpose(1, 0, 2)),
        "wobb": A(np.broadcast_to(wo_b, (128, D))),
        "a2t": A(lora_A.transpose(1, 0, 2).reshape(32, D).T.reshape(KT, 128, 32).transpose(1, 0, 2)),
        "b2": A(lora_B.transpose(2, 0, 1).reshape(32, FF)),
        "eye": np.eye(128, dtype=np.float32),
    }
    return {k: np.asarray(v, dtype=np.float32) for k, v in d.items()}


def kernel(**inputs):
    inputs = {k: np.asarray(v) for k, v in inputs.items()}
    x = np.ascontiguousarray(inputs["L0_attn_x"], dtype=np.float32)
    xf = x.reshape(8 * 2048, D)
    w = _prep(inputs)
    nc = build_nc()
    in_maps = []
    for c in range(8):
        m = dict(w)
        m["x"] = np.ascontiguousarray(xf[c * TPC:(c + 1) * TPC].reshape(NT, 128, D))
        in_maps.append(m)
    res = run_bass_kernel_spmd(nc, in_maps, core_ids=list(range(8)))
    out = np.concatenate([r["y"].reshape(TPC, D) for r in res.results], axis=0)
    return out.reshape(8, 2048, D).astype(np.float32)


if __name__ == "__main__":
    rng = np.random.default_rng(0)
    fake = {
        "L0_attn_x": rng.standard_normal((8, 2048, 1024), dtype=np.float32),
        "L0_attn_w_in": rng.standard_normal((3072, 1024), dtype=np.float32) * 0.02,
        "L0_attn_b_in": rng.standard_normal((3072,), dtype=np.float32) * 0.02,
        "L0_attn_w_out": rng.standard_normal((1024, 1024), dtype=np.float32) * 0.02,
        "L0_attn_b_out": rng.standard_normal((1024,), dtype=np.float32) * 0.02,
        "L0_ffn_gate_W": rng.standard_normal((8, 1024), dtype=np.float32) * 0.03,
        "L0_ffn_gate_b": rng.standard_normal((8,), dtype=np.float32) * 0.01,
        "L0_ffn_wi_W": rng.standard_normal((4096, 1024), dtype=np.float32) * 0.03,
        "L0_ffn_wi_b": rng.standard_normal((4096,), dtype=np.float32) * 0.01,
        "L0_ffn_wo_W": rng.standard_normal((1024, 4096), dtype=np.float32) * 0.015,
        "L0_ffn_wo_b": rng.standard_normal((1024,), dtype=np.float32) * 0.01,
        "L0_ffn_lora_A": rng.standard_normal((8, 4, 1024), dtype=np.float32) * 0.03,
        "L0_ffn_lora_B": rng.standard_normal((8, 4096, 4), dtype=np.float32) * 0.02,
    }
    out = kernel(**fake)
    print("out", out.shape, out.dtype, float(np.abs(out).mean()))


# version 10
# speedup vs baseline: 1.2582x; 1.0547x over previous
import numpy as np
import concourse.bass as bass
import concourse.mybir as mybir
from concourse.bass_utils import run_bass_kernel_spmd

F32 = mybir.dt.float32
BF16 = mybir.dt.bfloat16
AF = mybir.ActivationFunctionType
ALU = mybir.AluOpType
AX = mybir.AxisListType

D = 1024
FF = 4096
KT = 8
FT = 32
TPC = 2048
CH = 256
NCH = TPC // CH
JT = CH // 128
CHA = 512
NCHA = TPC // CHA
JTA = CHA // 128
NT = TPC // 128
EPS = 1e-5


class Buf:
    __slots__ = ("w", "r")

    def __init__(self):
        self.w = None
        self.r = []


class Bld:

    ENG_SECT = {"t": "tensor", "v": "vector", "s": "scalar", "sy": "sync", "g": "gpsimd"}

    def __init__(self):
        self.ops = {e: [] for e in self.ENG_SECT}
        self.n = {k: 0 for k in ("t", "v", "s", "qx", "qw", "qo")}
        self.waited = {}
        self.dummy_fn = {}
        self.dmy_buf = {"v": Buf(), "s": Buf()}

    def _wait(self, eng, dep):
        stream, val = dep
        if stream == eng:
            if eng == "t":
                return
        key = (eng, stream)
        if self.waited.get(key, 0) >= val:
            return
        self.waited[key] = val
        self.ops[eng].append(("wait", stream, val))

    def op(self, eng, fn, reads=(), writes=(), queue=None):
        if eng in ("v", "s") and self.dummy_fn.get(eng) is not None:
            last = (eng, self.n[eng])
            if any(b.w == last for b in reads):
                self.op(eng, self.dummy_fn[eng], reads=(), writes=(self.dmy_buf[eng],))
        for b in reads:
            if b.w is not None:
                self._wait(eng, b.w)
        for b in writes:
            if b.w is not None:
                self._wait(eng, b.w)
            for rd in b.r:
                self._wait(eng, rd)
        if queue is None:
            self.n[eng] += 1
            tok = (eng, self.n[eng])
            self.ops[eng].append(("do", fn, eng, 1))
        else:
            if self.n[queue] > 0:
                self._wait(eng, (queue, 16 * self.n[queue]))
            self.n[queue] += 1
            tok = (queue, 16 * self.n[queue])
            self.ops[eng].append(("do", fn, queue, 16))
        for b in reads:
            b.r.append(tok)
        for b in writes:
            b.w = tok
            b.r = []

    def raw_wait(self, eng, stream, val):
        key = (eng, stream)
        if self.waited.get(key, 0) >= val:
            return
        self.waited[key] = val
        self.ops[eng].append(("wait", stream, val))


def build_nc():
    nc = bass.Bass("TRN2", target_bir_lowering=False)

    xh = nc.dram_tensor("x", [NT, 128, D], F32, kind="ExternalInput")
    wvt_h = nc.dram_tensor("wvt", [128, KT, D], F32, kind="ExternalInput")
    wout_h = nc.dram_tensor("wout", [128, KT, D], F32, kind="ExternalInput")
    bv_h = nc.dram_tensor("bvp", [128, KT], F32, kind="ExternalInput")
    boutb_h = nc.dram_tensor("boutb", [128, D], F32, kind="ExternalInput")
    gwt_h = nc.dram_tensor("gwt", [128, KT, 8], F32, kind="ExternalInput")
    gb_h = nc.dram_tensor("gb", [8, 1], F32, kind="ExternalInput")
    wit_h = nc.dram_tensor("wit", [128, KT, FF], F32, kind="ExternalInput")
    wib_h = nc.dram_tensor("wibp", [128, FT], F32, kind="ExternalInput")
    wot_h = nc.dram_tensor("wot", [128, FT, D], F32, kind="ExternalInput")
    wobb_h = nc.dram_tensor("wobb", [128, D], F32, kind="ExternalInput")
    a2t_h = nc.dram_tensor("a2t", [128, KT, 32], F32, kind="ExternalInput")
    b2_h = nc.dram_tensor("b2", [32, FF], F32, kind="ExternalInput")
    eye_h = nc.dram_tensor("eye", [128, 128], F32, kind="ExternalInput")
    x1s_h = nc.dram_tensor("x1s", [NT, 128, D], F32, kind="Internal")
    yh = nc.dram_tensor("y", [NT, 128, D], F32, kind="ExternalOutput")

    layout = []

    def L(name, shape, dtype):
        layout.append((name, shape, dtype))
        return name

    static = [
        ("xsb0", [128, JTA, D], F32), ("xsb1", [128, JTA, D], F32),
        ("htmp", [128, 2, D], BF16),
        ("hT", [128, KT, CHA], BF16),
        ("trash", [128, D], BF16),
        ("eyebf", [128, 128], BF16),
        ("stats", [128, 80], F32),
        ("lg", [8, CH], BF16),
        ("lgT", [128, 16], BF16),
        ("maskT", [128, JT, 32], BF16),
        ("mask32", [32, CH], F32),
        ("xam", [32, CH], BF16),
        ("bv", [128, KT], F32),
        ("gateb", [8, 16], F32),
        ("wib", [128, FT], F32),
        ("a2t", [128, KT, 32], BF16),
        ("gwt", [128, KT, 8], BF16),
        ("if0", [128, CH], BF16), ("if1", [128, CH], BF16),
        ("dmyv", [128, 64], F32), ("dmys", [128, 64], F32),
        ("aoraw", [128, JTA, D], F32),
        ("wiT", [128, KT, FF], BF16),
        ("woTlo", [128, 16, D], BF16),
    ]
    dt_size = {F32: 4, BF16: 2}
    off = 0
    offs = {}
    for name, shape, dtype in static:
        per_part = int(np.prod(shape[1:])) * dt_size[dtype]
        offs[name] = off
        off += (per_part + 31) // 32 * 32
    alias_base = off
    alias = [
        ("wvT", [128, KT, D], BF16, 0),
        ("woutT", [128, KT, D], BF16, 16384),
        ("vT", [128, KT, CHA], BF16, 32768),
        ("boutb", [128, D], F32, 40960),
        ("woThi", [128, 16, D], BF16, 0),
        ("wobb", [128, D], F32, 32768),
        ("b2a", [32, FF // 2], BF16, 36864),
        ("b2b", [32, FF // 2], BF16, 40960),
    ]
    total = alias_base + 45056
    arena = nc.alloc_sbuf_tensor("arena", [128, total], mybir.dt.uint8)
    base = nc.lookup_mloc(arena).addr

    sb = {}
    for name, shape, dtype in static:
        sb[name] = nc.alloc_sbuf_tensor_at(name, shape, dtype, offset=base + offs[name])
    for name, shape, dtype, aoff in alias:
        sb[name] = nc.alloc_sbuf_tensor_at(name, shape, dtype, offset=base + alias_base + aoff)

    pT0 = nc.alloc_psum_tensor("pT0", [128, 1024], BF16)
    pV0 = nc.alloc_psum_tensor("pV0", [128, 512], F32)
    pV1 = nc.alloc_psum_tensor("pV1", [128, 512], F32)
    pW2 = nc.alloc_psum_tensor("pW2", [128, 512], F32)
    pO0 = nc.alloc_psum_tensor("pO0", [128, 1024], F32)
    pO1 = nc.alloc_psum_tensor("pO1", [128, 1024], F32)
    pT = [pT0, pT0]
    pV = [pV0, pV1]
    pWI = [pV1, pW2]
    pO = [pO0, pO1]

    B = Bld()
    B.dummy_fn["v"] = lambda v: v.tensor_copy(sb["dmyv"][:, 0:32], sb["dmyv"][:, 32:64])
    B.dummy_fn["s"] = lambda s: s.copy(sb["dmys"][:, 0:32], sb["dmys"][:, 32:64])
    xt = [[Buf() for _ in range(JTA)] for _ in range(2)]
    bhtmp = [Buf(), Buf()]
    bhT = Buf()
    vTb = [Buf() for _ in range(KT)]
    _bpT0 = Buf()
    bpT = [_bpT0, _bpT0]
    bpV = [Buf(), Buf()]
    bpWI = [bpV[1], Buf()]
    bpO = [Buf(), Buf()]
    bwv = [Buf(), Buf()]; bwo = Buf(); bgwt = Buf(); ba2t = Buf(); beye = Buf()
    bb2 = [Buf(), Buf()]
    bwiT = [Buf() for _ in range(4)]
    bwoT = [Buf() for _ in range(4)]
    bbv = Buf(); bboutb = Buf(); bgb = Buf(); bwib = Buf(); bwobb = Buf()
    bst = {k: Buf() for k in ("s", "q", "m", "e", "ms", "v", "sd", "r",
                              "s2", "q2", "m2", "e2", "ms2", "v2", "sd2", "r2", "mx")}
    blg = Buf(); blgT = Buf(); bmaskT = [Buf() for _ in range(JT)]
    bmask32 = Buf(); bxam = Buf()
    bif = [Buf(), Buf()]
    bx1s = [Buf() for _ in range(NT)]
    btrash = Buf()

    SC = {"s": 0, "q": 4, "m": 8, "e": 12, "ms": 16, "v": 20, "sd": 24, "r": 28,
          "s2": 32, "q2": 36, "m2": 40, "e2": 44, "ms2": 48, "v2": 52, "sd2": 56,
          "r2": 60, "mx": 64}

    st = sb["stats"]

    def stc(key, j=0, n=1):
        c = SC[key] + j
        return st[:, c:c + n]

    B.op("v", lambda v: v.memset(sb["dmyv"][:, :], 0.0), writes=[B.dmy_buf["v"]])
    B.op("s", lambda s: s.memzero(sb["dmys"][:, :]), writes=[B.dmy_buf["s"]])

    B.op("g", lambda g: g.dma_start(sb["eyebf"][:, :], eye_h[:, :]), writes=[beye], queue="qw")
    B.op("g", lambda g: g.dma_start(sb["wvT"][:, :, 0:512], wvt_h[:, :, 0:512]),
         writes=[bwv[0]], queue="qw")
    B.op("g", lambda g: g.dma_start(sb["wvT"][:, :, 512:1024], wvt_h[:, :, 512:1024]),
         writes=[bwv[1]], queue="qw")
    B.op("g", lambda g: g.dma_start(sb["woutT"][:, :, :], wout_h[:, :, :]), writes=[bwo], queue="qw")
    B.op("g", lambda g: g.dma_start(sb["gwt"][:, :, :], gwt_h[:, :, :]), writes=[bgwt], queue="qw")
    B.op("g", lambda g: g.dma_start(sb["a2t"][:, :, :], a2t_h[:, :, :]), writes=[ba2t], queue="qw")
    for gidx in range(4):
        k0 = 2 * gidx
        B.op("g", lambda g, k0=k0: g.dma_start(sb["wiT"][:, k0:k0 + 2, :], wit_h[:, k0:k0 + 2, :]),
             writes=[bwiT[gidx]], queue="qw")
    for gidx in range(2):
        f0 = 8 * gidx
        B.op("g", lambda g, f0=f0: g.dma_start(sb["woTlo"][:, f0:f0 + 8, :], wot_h[:, f0:f0 + 8, :]),
             writes=[bwoT[gidx]], queue="qw")
    def load_x(src, c, jt):
        cb = c % 2
        for j in range(jt):
            g = jt * c + j
            B.op("sy", lambda s, g=g, cb=cb, j=j: s.dma_start(sb["xsb%d" % cb][:, j, :], src[g]),
                 reads=([bx1s[g]] if src is x1s_h else []), writes=[xt[cb][j]], queue="qx")

    load_x(xh, 0, JTA)
    B.op("sy", lambda s: s.dma_start(sb["bv"][:, :], bv_h[:, :]), writes=[bbv], queue="qx")
    B.op("sy", lambda s: s.dma_start(sb["boutb"][:, :], boutb_h[:, :]), writes=[bboutb], queue="qx")
    B.op("sy", lambda s: s.dma_start(sb["gateb"][:, 0:1], gb_h[:, :]), writes=[bgb], queue="qx")
    B.op("sy", lambda s: s.dma_start(sb["wib"][:, :], wib_h[:, :]), writes=[bwib], queue="qx")

    def ln_batched(cb, inp_bufs, jt):
        xs = sb["xsb%d" % cb]
        hm = sb["htmp"]
        for j in range(jt):
            B.op("v", lambda v, j=j, xs=xs: v.reduce_sum(stc("s", j), xs[:, j, :], axis=AX.X),
                 reads=[inp_bufs[j]], writes=[bst["s"]])
        for j in range(jt):
            B.op("s", lambda s, j=j, xs=xs: s.activation(sb["trash"][:, :], xs[:, j, :], AF.Square,
                                                         bias=0.0, scale=1.0, accum_out=stc("q", j)),
                 reads=[inp_bufs[j]], writes=[btrash, bst["q"]])
        B.op("v", lambda v: v.tensor_scalar_mul(stc("m", 0, jt), stc("s", 0, jt), 1.0 / D),
             reads=[bst["s"]], writes=[bst["m"]])
        B.op("v", lambda v: v.tensor_scalar_mul(stc("e", 0, jt), stc("q", 0, jt), 1.0 / D),
             reads=[bst["q"]], writes=[bst["e"]])
        B.op("v", lambda v: v.tensor_mul(stc("ms", 0, jt), stc("m", 0, jt), stc("m", 0, jt)),
             reads=[bst["m"]], writes=[bst["ms"]])
        B.op("v", lambda v: v.scalar_tensor_tensor(stc("v", 0, jt), stc("e", 0, jt), EPS,
                                                   stc("ms", 0, jt), ALU.add, ALU.subtract),
             reads=[bst["e"], bst["ms"]], writes=[bst["v"]])
        B.op("s", lambda s: s.activation(stc("sd", 0, jt), stc("v", 0, jt), AF.Sqrt, bias=0.0, scale=1.0),
             reads=[bst["v"]], writes=[bst["sd"]])
        B.op("v", lambda v: v.reciprocal(stc("r", 0, jt), stc("sd", 0, jt)),
             reads=[bst["sd"]], writes=[bst["r"]])

    def transposes(cb, jt):
        xs = sb["xsb%d" % cb]
        hm = sb["htmp"]
        for j in range(jt):
            B.op("v", lambda v, j=j, xs=xs, hm=hm: v.tensor_scalar(
                hm[:, j % 2, :], xs[:, j, :], stc("m", j), stc("r", j), ALU.subtract, ALU.mult),
                reads=[xt[cb][j], bst["m"], bst["r"]], writes=[bhtmp[j % 2]])
            for dt in range(KT):
                B.op("t", lambda t, j=j, dt=dt, hm=hm: t.transpose(
                    pT[0][:, dt * 128:(dt + 1) * 128],
                    hm[:, j % 2, dt * 128:(dt + 1) * 128], sb["eyebf"][:, :]),
                    reads=[bhtmp[j % 2], beye], writes=[bpT[0]])
            B.op("s", lambda s, j=j: s.copy(
                sb["hT"][:, :, j * 128:(j + 1) * 128],
                pT[0][:, :].rearrange("p (a b) -> p a b", b=128)),
                reads=[bpT[0]], writes=[bhT])

    baoraw = [Buf() for _ in range(JTA)]
    baotmp = Buf()

    def epi_quick(pOb, j_slot):
        B.op("s", lambda s, pOb=pOb, j_slot=j_slot: s.copy(sb["aoraw"][:, j_slot, :], pO[pOb][:, :]),
             reads=[bpO[pOb]], writes=[baoraw[j_slot]])

    def epi_batch(cb, jt, bias_name, bias_buf, store_dst, g0, sq_on_vector):
        xs = sb["xsb%d" % cb]
        ao = sb["aoraw"]
        for j in range(jt):
            B.op("v", lambda v, j=j, ao=ao: v.scalar_tensor_tensor(
                ao[:, j, :], ao[:, j, :], 1.0, sb[bias_name][:, :],
                ALU.mult, ALU.add, accum_out=stc("s2", j)),
                reads=[baoraw[j], bias_buf], writes=[baoraw[j], bst["s2"]])
        for j in range(jt):
            if sq_on_vector:
                B.op("v", lambda v, j=j, ao=ao: v.scalar_tensor_tensor(
                    sb["trash"][:, :], ao[:, j, :], 1.0, ao[:, j, :],
                    ALU.mult, ALU.mult, accum_out=stc("q2", j)),
                    reads=[baoraw[j]], writes=[btrash, bst["q2"]])
            else:
                B.op("s", lambda s, j=j, ao=ao: s.activation(
                    sb["trash"][:, :], ao[:, j, :], AF.Square,
                    bias=0.0, scale=1.0, accum_out=stc("q2", j)),
                    reads=[baoraw[j]], writes=[btrash, bst["q2"]])
        B.op("v", lambda v: v.tensor_scalar_mul(stc("m2", 0, jt), stc("s2", 0, jt), 1.0 / D),
             reads=[bst["s2"]], writes=[bst["m2"]])
        B.op("v", lambda v: v.tensor_scalar_mul(stc("e2", 0, jt), stc("q2", 0, jt), 1.0 / D),
             reads=[bst["q2"]], writes=[bst["e2"]])
        B.op("v", lambda v: v.tensor_mul(stc("ms2", 0, jt), stc("m2", 0, jt), stc("m2", 0, jt)),
             reads=[bst["m2"]], writes=[bst["ms2"]])
        B.op("v", lambda v: v.scalar_tensor_tensor(stc("v2", 0, jt), stc("e2", 0, jt), EPS,
                                                   stc("ms2", 0, jt), ALU.add, ALU.subtract),
             reads=[bst["e2"], bst["ms2"]], writes=[bst["v2"]])
        B.op("s", lambda s: s.activation(stc("sd2", 0, jt), stc("v2", 0, jt), AF.Sqrt,
                                         bias=0.0, scale=1.0),
             reads=[bst["v2"]], writes=[bst["sd2"]])
        B.op("v", lambda v: v.reciprocal(stc("r2", 0, jt), stc("sd2", 0, jt)),
             reads=[bst["sd2"]], writes=[bst["r2"]])
        for j in range(jt):
            B.op("v", lambda v, j=j, ao=ao: v.tensor_scalar(
                ao[:, j, :], ao[:, j, :], stc("m2", j), stc("r2", j), ALU.subtract, ALU.mult),
                reads=[baoraw[j], bst["m2"], bst["r2"]], writes=[baoraw[j]])
            B.op("v", lambda v, j=j, xs=xs, ao=ao: v.tensor_add(xs[:, j, :], xs[:, j, :], ao[:, j, :]),
                 reads=[xt[cb][j], baoraw[j]], writes=[xt[cb][j]])
            g = g0 + j
            extra = [bx1s[g]] if store_dst is x1s_h else []
            if j % 2 == 0:
                B.op("sy", lambda s, g=g, j=j, xs=xs: s.dma_start(store_dst[g], xs[:, j, :]),
                     reads=[xt[cb][j]], writes=extra, queue="qx")
            else:
                B.op("s", lambda s, g=g, j=j, xs=xs: s.dma_start(store_dst[g], xs[:, j, :]),
                     reads=[xt[cb][j]], writes=extra, queue="qo")

    for c in range(NCHA):
        cb = c % 2
        ln_batched(cb, xt[cb], JTA)
        transposes(cb, JTA)
        for dt in range(KT):
            pb = dt % 2
            for k in range(KT):
                B.op("t", lambda t, pb=pb, k=k, dt=dt: t.matmul(
                    pV[pb][:, 0:CHA], sb["wvT"][:, k, dt * 128:(dt + 1) * 128], sb["hT"][:, k, :],
                    start=(k == 0), stop=(k == KT - 1)),
                    reads=[bwv[dt // 4], bhT], writes=[bpV[pb]])
            B.op("s", lambda s, pb=pb, dt=dt: s.activation(
                sb["vT"][:, dt, :], pV[pb][:, 0:CHA], AF.Identity,
                bias=sb["bv"][:, dt:dt + 1], scale=1.0),
                reads=[bpV[pb], bbv], writes=[vTb[dt]])
        if c >= 1:
            epi_batch((c - 1) % 2, JTA, "boutb", bboutb, x1s_h, JTA * (c - 1), sq_on_vector=False)
        if c + 1 < NCHA:
            load_x(xh, c + 1, JTA)
        for j in range(JTA):
            for nch in range(2):
                for k in range(KT):
                    B.op("t", lambda t, j=j, nch=nch, k=k: t.matmul(
                        pO[j % 2][:, nch * 512:(nch + 1) * 512],
                        sb["vT"][:, k, j * 128:(j + 1) * 128],
                        sb["woutT"][:, k, nch * 512:(nch + 1) * 512],
                        start=(k == 0), stop=(k == KT - 1)),
                        reads=[vTb[k], bwo], writes=[bpO[j % 2]])
            epi_quick(j % 2, j)
    epi_batch((NCHA - 1) % 2, JTA, "boutb", bboutb, x1s_h, JTA * (NCHA - 1), sq_on_vector=False)

    B.op("g", lambda g: g.dma_start(sb["b2a"][:, :], b2_h[:, 0:FF // 2]),
         writes=[bb2[0]] + vTb, queue="qw")
    B.op("g", lambda g: g.dma_start(sb["b2b"][:, :], b2_h[:, FF // 2:FF]),
         writes=[bb2[1], bboutb] + vTb, queue="qw")
    B.op("g", lambda g: g.dma_start(sb["woThi"][:, 0:8, :], wot_h[:, 16:24, :]),
         writes=[bwoT[2]] + bwv, queue="qw")
    B.op("g", lambda g: g.dma_start(sb["woThi"][:, 8:16, :], wot_h[:, 24:32, :]),
         writes=[bwoT[3], bwo], queue="qw")
    B.op("sy", lambda s: s.dma_start(sb["wobb"][:, :], wobb_h[:, :]),
         writes=[bwobb] + vTb, queue="qx")

    load_x(x1s_h, 0, JT)
    for c in range(NCH):
        cb = c % 2
        ln_batched(cb, xt[cb], JT)
        transposes(cb, JT)
        for k in range(KT):
            B.op("t", lambda t, k=k: t.matmul(pV[0][0:8, 0:CH], sb["gwt"][:, k, :], sb["hT"][:, k, 0:CH],
                                              start=(k == 0), stop=(k == KT - 1)),
                 reads=[bgwt, bhT], writes=[bpV[0]])
        for k in range(KT):
            B.op("t", lambda t, k=k: t.matmul(pV[0][0:32, 256:256 + CH], sb["a2t"][:, k, :],
                                              sb["hT"][:, k, 0:CH], start=(k == 0), stop=(k == KT - 1)),
                 reads=[ba2t, bhT], writes=[bpV[0]])
        B.op("s", lambda s: s.activation(sb["lg"][:, :], pV[0][0:8, 0:CH], AF.Identity,
                                         bias=sb["gateb"][:, 0:1], scale=1.0),
             reads=[bpV[0], bgb], writes=[blg])
        for j in range(JT):
            B.op("t", lambda t, j=j: t.transpose(pT[1][:, j * 8:(j + 1) * 8],
                                                 sb["lg"][:, j * 128:(j + 1) * 128],
                                                 sb["eyebf"][0:8, 0:8]),
                 reads=[blg, beye], writes=[bpT[1]])
            B.op("s", lambda s, j=j: s.copy(sb["lgT"][:, j * 8:(j + 1) * 8], pT[1][:, j * 8:(j + 1) * 8]),
                 reads=[bpT[1]], writes=[blgT])
            B.op("v", lambda v, j=j: v.reduce_max(stc("mx", j), sb["lgT"][:, j * 8:(j + 1) * 8], axis=AX.X),
                 reads=[blgT], writes=[bst["mx"]])
            for r in range(4):
                B.op("v", lambda v, j=j, r=r: v.tensor_scalar(
                    sb["maskT"][:, j, r * 8:(r + 1) * 8], sb["lgT"][:, j * 8:(j + 1) * 8],
                    stc("mx", j), None, ALU.is_ge),
                    reads=[blgT, bst["mx"]], writes=[bmaskT[j]])
            B.op("t", lambda t, j=j: t.transpose(pT[1][0:32, 128 + j * 128:256 + j * 128],
                                                 sb["maskT"][:, j, :], sb["eyebf"][:, :]),
                 reads=[bmaskT[j], beye], writes=[bpT[1]])
            B.op("s", lambda s, j=j: s.copy(sb["mask32"][0:32, j * 128:(j + 1) * 128],
                                            pT[1][0:32, 128 + j * 128:256 + j * 128]),
                 reads=[bpT[1]], writes=[bmask32])
        B.op("v", lambda v: v.tensor_mul(sb["xam"][0:32, :], pV[0][0:32, 256:256 + CH],
                                         sb["mask32"][0:32, :]),
             reads=[bpV[0], bmask32], writes=[bxam])
        def wo_emit(ft):
            ib = ft % 2
            wo_sb = sb["woTlo"] if ft < 16 else sb["woThi"]
            fl = ft if ft < 16 else ft - 16
            for j in range(JT):
                for nch in range(2):
                    B.op("t", lambda t, ib=ib, j=j, nch=nch, wo_sb=wo_sb, fl=fl, ft=ft: t.matmul(
                        pO[j][:, nch * 512:(nch + 1) * 512],
                        sb["if%d" % ib][:, j * 128:(j + 1) * 128],
                        wo_sb[:, fl, nch * 512:(nch + 1) * 512],
                        start=(ft == 0), stop=(ft == FT - 1)),
                        reads=[bif[ib], bwoT[ft // 8]], writes=[bpO[j]])

        for ft in range(FT):
            if ft == 8:
                if c >= 1:
                    epi_batch((c - 1) % 2, JT, "wobb", bwobb, yh, JT * (c - 1), sq_on_vector=True)
                if c + 1 < NCH:
                    load_x(x1s_h, c + 1, JT)
            ib = ft % 2
            pw = pWI[ib]
            bpw = bpWI[ib]
            for k in range(KT):
                B.op("t", lambda t, k=k, ft=ft, pw=pw: t.matmul(
                    pw[:, 0:CH], sb["wiT"][:, k, ft * 128:(ft + 1) * 128], sb["hT"][:, k, 0:CH],
                    start=(k == 0), stop=False),
                    reads=[bwiT[k // 2], bhT], writes=[bpw])
            b2t = sb["b2a"] if ft < 16 else sb["b2b"]
            fb = ft if ft < 16 else ft - 16
            B.op("t", lambda t, pw=pw, b2t=b2t, fb=fb: t.matmul(pw[:, 0:CH], b2t[0:32, fb * 128:(fb + 1) * 128],
                                                       sb["xam"][0:32, :], start=False, stop=True),
                 reads=[bb2[0 if ft < 16 else 1], bxam], writes=[bpw])
            B.op("s", lambda s, ib=ib, ft=ft, pw=pw: s.activation(
                sb["if%d" % ib][:, :], pw[:, 0:CH], AF.Relu,
                bias=sb["wib"][:, ft:ft + 1], scale=1.0),
                reads=[bpw, bwib], writes=[bif[ib]])
            if ft >= 1:
                wo_emit(ft - 1)
        wo_emit(FT - 1)
        for j in range(JT):
            epi_quick(j, j)
    epi_batch((NCH - 1) % 2, JT, "wobb", bwobb, yh, JT * (NCH - 1), sq_on_vector=True)

    B.raw_wait("sy", "qx", 16 * B.n["qx"])

    with (
        nc.semaphore("sem_t") as sem_t,
        nc.semaphore("sem_v") as sem_v,
        nc.semaphore("sem_s") as sem_s,
        nc.semaphore("sem_qx") as sem_qx,
        nc.semaphore("sem_qw") as sem_qw,
        nc.semaphore("sem_qo") as sem_qo,
        nc.Block() as block,
    ):
        sems = {"t": sem_t, "v": sem_v, "s": sem_s, "qx": sem_qx, "qw": sem_qw, "qo": sem_qo}

        def replay(eng_key):
            def run(h):
                for item in B.ops[eng_key]:
                    if item[0] == "wait":
                        h.wait_ge(sems[item[1]], item[2])
                    else:
                        _, fn, semkey, inc = item
                        fn(h).then_inc(sems[semkey], inc)
            return run

        @block.tensor
        def _(t):
            replay("t")(t)

        @block.vector
        def _(v):
            replay("v")(v)

        @block.scalar
        def _(s):
            replay("s")(s)

        @block.sync
        def _(sy):
            replay("sy")(sy)

        @block.gpsimd
        def _(g):
            replay("g")(g)

    return nc


def _prep(inp):
    w_in = inp["L0_attn_w_in"]
    b_in = inp["L0_attn_b_in"]
    w_out = inp["L0_attn_w_out"]
    b_out = inp["L0_attn_b_out"]
    gate_W = inp["L0_ffn_gate_W"]
    gate_b = inp["L0_ffn_gate_b"]
    wi_W = inp["L0_ffn_wi_W"]
    wi_b = inp["L0_ffn_wi_b"]
    wo_W = inp["L0_ffn_wo_W"]
    wo_b = inp["L0_ffn_wo_b"]
    lora_A = inp["L0_ffn_lora_A"]
    lora_B = inp["L0_ffn_lora_B"]

    A = np.ascontiguousarray
    wv = w_in.reshape(16, 3, 64, D)[:, 2].reshape(D, D)
    d = {
        "wvt": A(wv.T.reshape(KT, 128, D).transpose(1, 0, 2)),
        "wout": A(w_out.T.reshape(KT, 128, D).transpose(1, 0, 2)),
        "bvp": A(b_in.reshape(16, 3, 64)[:, 2].reshape(KT, 128).T),
        "boutb": A(np.broadcast_to(b_out, (128, D))),
        "gwt": A(gate_W.T.reshape(KT, 128, 8).transpose(1, 0, 2)),
        "gb": A(gate_b.reshape(8, 1)),
        "wit": A(wi_W.T.reshape(KT, 128, FF).transpose(1, 0, 2)),
        "wibp": A(wi_b.reshape(FT, 128).T),
        "wot": A(wo_W.T.reshape(FT, 128, D).transpose(1, 0, 2)),
        "wobb": A(np.broadcast_to(wo_b, (128, D))),
        "a2t": A(lora_A.transpose(1, 0, 2).reshape(32, D).T.reshape(KT, 128, 32).transpose(1, 0, 2)),
        "b2": A(lora_B.transpose(2, 0, 1).reshape(32, FF)),
        "eye": np.eye(128, dtype=np.float32),
    }
    return {k: np.asarray(v, dtype=np.float32) for k, v in d.items()}


def kernel(**inputs):
    inputs = {k: np.asarray(v) for k, v in inputs.items()}
    x = np.ascontiguousarray(inputs["L0_attn_x"], dtype=np.float32)
    xf = x.reshape(8 * 2048, D)
    w = _prep(inputs)
    nc = build_nc()
    in_maps = []
    for c in range(8):
        m = dict(w)
        m["x"] = np.ascontiguousarray(xf[c * TPC:(c + 1) * TPC].reshape(NT, 128, D))
        in_maps.append(m)
    res = run_bass_kernel_spmd(nc, in_maps, core_ids=list(range(8)))
    out = np.concatenate([r["y"].reshape(TPC, D) for r in res.results], axis=0)
    return out.reshape(8, 2048, D).astype(np.float32)


if __name__ == "__main__":
    rng = np.random.default_rng(0)
    fake = {
        "L0_attn_x": rng.standard_normal((8, 2048, 1024), dtype=np.float32),
        "L0_attn_w_in": rng.standard_normal((3072, 1024), dtype=np.float32) * 0.02,
        "L0_attn_b_in": rng.standard_normal((3072,), dtype=np.float32) * 0.02,
        "L0_attn_w_out": rng.standard_normal((1024, 1024), dtype=np.float32) * 0.02,
        "L0_attn_b_out": rng.standard_normal((1024,), dtype=np.float32) * 0.02,
        "L0_ffn_gate_W": rng.standard_normal((8, 1024), dtype=np.float32) * 0.03,
        "L0_ffn_gate_b": rng.standard_normal((8,), dtype=np.float32) * 0.01,
        "L0_ffn_wi_W": rng.standard_normal((4096, 1024), dtype=np.float32) * 0.03,
        "L0_ffn_wi_b": rng.standard_normal((4096,), dtype=np.float32) * 0.01,
        "L0_ffn_wo_W": rng.standard_normal((1024, 4096), dtype=np.float32) * 0.015,
        "L0_ffn_wo_b": rng.standard_normal((1024,), dtype=np.float32) * 0.01,
        "L0_ffn_lora_A": rng.standard_normal((8, 4, 1024), dtype=np.float32) * 0.03,
        "L0_ffn_lora_B": rng.standard_normal((8, 4096, 4), dtype=np.float32) * 0.02,
    }
    out = kernel(**fake)
    print("out", out.shape, out.dtype, float(np.abs(out).mean()))


# version 11
# speedup vs baseline: 1.2627x; 1.0036x over previous
import numpy as np
import concourse.bass as bass
import concourse.mybir as mybir
from concourse.bass_utils import run_bass_kernel_spmd

F32 = mybir.dt.float32
BF16 = mybir.dt.bfloat16
AF = mybir.ActivationFunctionType
ALU = mybir.AluOpType
AX = mybir.AxisListType

D = 1024
FF = 4096
KT = 8
FT = 32
TPC = 2048
CH = 256
NCH = TPC // CH
JT = CH // 128
CHA = 512
NCHA = TPC // CHA
JTA = CHA // 128
NT = TPC // 128
EPS = 1e-5


class Buf:
    __slots__ = ("w", "r")

    def __init__(self):
        self.w = None
        self.r = []


class Bld:

    ENG_SECT = {"t": "tensor", "v": "vector", "s": "scalar", "sy": "sync", "g": "gpsimd"}

    def __init__(self):
        self.ops = {e: [] for e in self.ENG_SECT}
        self.n = {k: 0 for k in ("t", "v", "s", "qx", "qw", "qo")}
        self.waited = {}
        self.dummy_fn = {}
        self.dmy_buf = {"v": Buf(), "s": Buf()}

    def _wait(self, eng, dep):
        stream, val = dep
        if stream == eng:
            if eng == "t":
                return
        key = (eng, stream)
        if self.waited.get(key, 0) >= val:
            return
        self.waited[key] = val
        self.ops[eng].append(("wait", stream, val))

    def op(self, eng, fn, reads=(), writes=(), queue=None):
        if eng in ("v", "s") and self.dummy_fn.get(eng) is not None:
            last = (eng, self.n[eng])
            if any(b.w == last for b in reads):
                self.op(eng, self.dummy_fn[eng], reads=(), writes=(self.dmy_buf[eng],))
        for b in reads:
            if b.w is not None:
                self._wait(eng, b.w)
        for b in writes:
            if b.w is not None:
                self._wait(eng, b.w)
            for rd in b.r:
                self._wait(eng, rd)
        if queue is None:
            self.n[eng] += 1
            tok = (eng, self.n[eng])
            self.ops[eng].append(("do", fn, eng, 1))
        else:
            if self.n[queue] > 0:
                self._wait(eng, (queue, 16 * self.n[queue]))
            self.n[queue] += 1
            tok = (queue, 16 * self.n[queue])
            self.ops[eng].append(("do", fn, queue, 16))
        for b in reads:
            b.r.append(tok)
        for b in writes:
            b.w = tok
            b.r = []

    def raw_wait(self, eng, stream, val):
        key = (eng, stream)
        if self.waited.get(key, 0) >= val:
            return
        self.waited[key] = val
        self.ops[eng].append(("wait", stream, val))


def build_nc():
    nc = bass.Bass("TRN2", target_bir_lowering=False)

    xh = nc.dram_tensor("x", [NT, 128, D], F32, kind="ExternalInput")
    wvt_h = nc.dram_tensor("wvt", [128, KT, D], F32, kind="ExternalInput")
    wout_h = nc.dram_tensor("wout", [128, KT, D], F32, kind="ExternalInput")
    bv_h = nc.dram_tensor("bvp", [128, KT], F32, kind="ExternalInput")
    boutb_h = nc.dram_tensor("boutb", [128, D], F32, kind="ExternalInput")
    gwt_h = nc.dram_tensor("gwt", [128, KT, 8], F32, kind="ExternalInput")
    gb_h = nc.dram_tensor("gb", [8, 1], F32, kind="ExternalInput")
    wit_h = nc.dram_tensor("wit", [128, KT, FF], F32, kind="ExternalInput")
    wib_h = nc.dram_tensor("wibp", [128, FT], F32, kind="ExternalInput")
    wot_h = nc.dram_tensor("wot", [128, FT, D], F32, kind="ExternalInput")
    wobb_h = nc.dram_tensor("wobb", [128, D], F32, kind="ExternalInput")
    a2t_h = nc.dram_tensor("a2t", [128, KT, 32], F32, kind="ExternalInput")
    b2_h = nc.dram_tensor("b2", [32, FF], F32, kind="ExternalInput")
    eye_h = nc.dram_tensor("eye", [128, 128], F32, kind="ExternalInput")
    x1s_h = nc.dram_tensor("x1s", [NT, 128, D], F32, kind="Internal")
    yh = nc.dram_tensor("y", [NT, 128, D], F32, kind="ExternalOutput")

    layout = []

    def L(name, shape, dtype):
        layout.append((name, shape, dtype))
        return name

    static = [
        ("xsb0", [128, JTA, D], F32), ("xsb1", [128, JTA, D], F32),
        ("htmp", [128, 2, D], BF16),
        ("hT", [128, KT, CHA], BF16),
        ("trash", [128, D], BF16),
        ("eyebf", [128, 128], BF16),
        ("stats", [128, 80], F32),
        ("lg", [8, CH], BF16),
        ("lgT", [128, 16], BF16),
        ("maskT", [128, JT, 32], BF16),
        ("mask32", [32, CH], F32),
        ("xam", [32, CH], BF16),
        ("bv", [128, KT], F32),
        ("gateb", [8, 16], F32),
        ("wib", [128, FT], F32),
        ("a2t", [128, KT, 32], BF16),
        ("gwt", [128, KT, 8], BF16),
        ("if0", [128, CH], BF16), ("if1", [128, CH], BF16),
        ("dmyv", [128, 64], F32), ("dmys", [128, 64], F32),
        ("aoraw", [128, JTA, D], F32),
        ("wiT", [128, KT, FF], BF16),
        ("woTlo", [128, 16, D], BF16),
    ]
    dt_size = {F32: 4, BF16: 2}
    off = 0
    offs = {}
    for name, shape, dtype in static:
        per_part = int(np.prod(shape[1:])) * dt_size[dtype]
        offs[name] = off
        off += (per_part + 31) // 32 * 32
    alias_base = off
    alias = [
        ("wvT", [128, KT, D], BF16, 0),
        ("woutT", [128, KT, D], BF16, 16384),
        ("vT", [128, KT, CHA], BF16, 32768),
        ("boutb", [128, D], F32, 40960),
        ("woThi", [128, 16, D], BF16, 0),
        ("wobb", [128, D], F32, 32768),
        ("b2a", [32, FF // 2], BF16, 36864),
        ("b2b", [32, FF // 2], BF16, 40960),
    ]
    total = alias_base + 45056
    arena = nc.alloc_sbuf_tensor("arena", [128, total], mybir.dt.uint8)
    base = nc.lookup_mloc(arena).addr

    sb = {}
    for name, shape, dtype in static:
        sb[name] = nc.alloc_sbuf_tensor_at(name, shape, dtype, offset=base + offs[name])
    for name, shape, dtype, aoff in alias:
        sb[name] = nc.alloc_sbuf_tensor_at(name, shape, dtype, offset=base + alias_base + aoff)

    pT0 = nc.alloc_psum_tensor("pT0", [128, 1024], BF16)
    pV0 = nc.alloc_psum_tensor("pV0", [128, 512], F32)
    pV1 = nc.alloc_psum_tensor("pV1", [128, 512], F32)
    pW2 = nc.alloc_psum_tensor("pW2", [128, 512], F32)
    pO0 = nc.alloc_psum_tensor("pO0", [128, 1024], F32)
    pO1 = nc.alloc_psum_tensor("pO1", [128, 1024], F32)
    pT = [pT0, pT0]
    pV = [pV0, pV1]
    pWI = [pV1, pW2]
    pO = [pO0, pO1]

    B = Bld()
    B.dummy_fn["v"] = lambda v: v.tensor_copy(sb["dmyv"][:, 0:32], sb["dmyv"][:, 32:64])
    B.dummy_fn["s"] = lambda s: s.copy(sb["dmys"][:, 0:32], sb["dmys"][:, 32:64])
    xt = [[Buf() for _ in range(JTA)] for _ in range(2)]
    bhtmp = [Buf(), Buf()]
    bhT = Buf()
    vTb = [Buf() for _ in range(KT)]
    _bpT0 = Buf()
    bpT = [_bpT0, _bpT0]
    bpV = [Buf(), Buf()]
    bpWI = [bpV[1], Buf()]
    bpO = [Buf(), Buf()]
    bwv = [Buf(), Buf()]; bwo = Buf(); bgwt = Buf(); ba2t = Buf(); beye = Buf()
    bb2 = [Buf(), Buf()]
    bwiT = [Buf() for _ in range(4)]
    bwoT = [Buf() for _ in range(4)]
    bbv = Buf(); bboutb = Buf(); bgb = Buf(); bwib = Buf(); bwobb = Buf()
    bst = {k: Buf() for k in ("s", "q", "m", "e", "ms", "v", "sd", "r",
                              "s2", "q2", "m2", "e2", "ms2", "v2", "sd2", "r2", "mx")}
    blg = Buf(); blgT = Buf(); bmaskT = [Buf() for _ in range(JT)]
    bmask32 = Buf(); bxam = Buf()
    bif = [Buf(), Buf()]
    bx1s = [Buf() for _ in range(NT)]
    btrash = Buf()

    SC = {"s": 0, "q": 4, "m": 8, "e": 12, "ms": 16, "v": 20, "sd": 24, "r": 28,
          "s2": 32, "q2": 36, "m2": 40, "e2": 44, "ms2": 48, "v2": 52, "sd2": 56,
          "r2": 60, "mx": 64}

    st = sb["stats"]

    def stc(key, j=0, n=1):
        c = SC[key] + j
        return st[:, c:c + n]

    B.op("v", lambda v: v.memset(sb["dmyv"][:, :], 0.0), writes=[B.dmy_buf["v"]])
    B.op("s", lambda s: s.memzero(sb["dmys"][:, :]), writes=[B.dmy_buf["s"]])

    B.op("g", lambda g: g.dma_start(sb["eyebf"][:, :], eye_h[:, :]), writes=[beye], queue="qw")
    B.op("g", lambda g: g.dma_start(sb["wvT"][:, :, 0:512], wvt_h[:, :, 0:512]),
         writes=[bwv[0]], queue="qw")
    B.op("g", lambda g: g.dma_start(sb["wvT"][:, :, 512:1024], wvt_h[:, :, 512:1024]),
         writes=[bwv[1]], queue="qw")
    B.op("g", lambda g: g.dma_start(sb["woutT"][:, :, :], wout_h[:, :, :]), writes=[bwo], queue="qw")
    B.op("g", lambda g: g.dma_start(sb["gwt"][:, :, :], gwt_h[:, :, :]), writes=[bgwt], queue="qw")
    B.op("g", lambda g: g.dma_start(sb["a2t"][:, :, :], a2t_h[:, :, :]), writes=[ba2t], queue="qw")
    for gidx in range(4):
        k0 = 2 * gidx
        B.op("g", lambda g, k0=k0: g.dma_start(sb["wiT"][:, k0:k0 + 2, :], wit_h[:, k0:k0 + 2, :]),
             writes=[bwiT[gidx]], queue="qw")
    for gidx in range(2):
        f0 = 8 * gidx
        B.op("g", lambda g, f0=f0: g.dma_start(sb["woTlo"][:, f0:f0 + 8, :], wot_h[:, f0:f0 + 8, :]),
             writes=[bwoT[gidx]], queue="qw")
    def load_x(src, c, jt):
        cb = c % 2
        for j in range(jt):
            g = jt * c + j
            eng, q = ("sy", "qx") if j % 2 == 0 else ("s", "qo")
            B.op(eng, lambda s, g=g, cb=cb, j=j: s.dma_start(sb["xsb%d" % cb][:, j, :], src[g]),
                 reads=([bx1s[g]] if src is x1s_h else []), writes=[xt[cb][j]], queue=q)

    load_x(xh, 0, JTA)
    B.op("sy", lambda s: s.dma_start(sb["bv"][:, :], bv_h[:, :]), writes=[bbv], queue="qx")
    B.op("sy", lambda s: s.dma_start(sb["boutb"][:, :], boutb_h[:, :]), writes=[bboutb], queue="qx")
    B.op("sy", lambda s: s.dma_start(sb["gateb"][:, 0:1], gb_h[:, :]), writes=[bgb], queue="qx")
    B.op("sy", lambda s: s.dma_start(sb["wib"][:, :], wib_h[:, :]), writes=[bwib], queue="qx")

    def ln_batched(cb, inp_bufs, jt):
        xs = sb["xsb%d" % cb]
        hm = sb["htmp"]
        for j in range(jt):
            B.op("v", lambda v, j=j, xs=xs: v.reduce_sum(stc("s", j), xs[:, j, :], axis=AX.X),
                 reads=[inp_bufs[j]], writes=[bst["s"]])
        for j in range(jt):
            B.op("s", lambda s, j=j, xs=xs: s.activation(sb["trash"][:, :], xs[:, j, :], AF.Square,
                                                         bias=0.0, scale=1.0, accum_out=stc("q", j)),
                 reads=[inp_bufs[j]], writes=[btrash, bst["q"]])
        B.op("v", lambda v: v.tensor_scalar_mul(stc("m", 0, jt), stc("s", 0, jt), 1.0 / D),
             reads=[bst["s"]], writes=[bst["m"]])
        B.op("v", lambda v: v.tensor_scalar_mul(stc("e", 0, jt), stc("q", 0, jt), 1.0 / D),
             reads=[bst["q"]], writes=[bst["e"]])
        B.op("v", lambda v: v.tensor_mul(stc("ms", 0, jt), stc("m", 0, jt), stc("m", 0, jt)),
             reads=[bst["m"]], writes=[bst["ms"]])
        B.op("v", lambda v: v.scalar_tensor_tensor(stc("v", 0, jt), stc("e", 0, jt), EPS,
                                                   stc("ms", 0, jt), ALU.add, ALU.subtract),
             reads=[bst["e"], bst["ms"]], writes=[bst["v"]])
        B.op("s", lambda s: s.activation(stc("sd", 0, jt), stc("v", 0, jt), AF.Sqrt, bias=0.0, scale=1.0),
             reads=[bst["v"]], writes=[bst["sd"]])
        B.op("v", lambda v: v.reciprocal(stc("r", 0, jt), stc("sd", 0, jt)),
             reads=[bst["sd"]], writes=[bst["r"]])

    def transposes(cb, jt):
        xs = sb["xsb%d" % cb]
        hm = sb["htmp"]
        for j in range(jt):
            B.op("v", lambda v, j=j, xs=xs, hm=hm: v.tensor_scalar(
                hm[:, j % 2, :], xs[:, j, :], stc("m", j), stc("r", j), ALU.subtract, ALU.mult),
                reads=[xt[cb][j], bst["m"], bst["r"]], writes=[bhtmp[j % 2]])
            for dt in range(KT):
                B.op("t", lambda t, j=j, dt=dt, hm=hm: t.transpose(
                    pT[0][:, dt * 128:(dt + 1) * 128],
                    hm[:, j % 2, dt * 128:(dt + 1) * 128], sb["eyebf"][:, :]),
                    reads=[bhtmp[j % 2], beye], writes=[bpT[0]])
            B.op("s", lambda s, j=j: s.copy(
                sb["hT"][:, :, j * 128:(j + 1) * 128],
                pT[0][:, :].rearrange("p (a b) -> p a b", b=128)),
                reads=[bpT[0]], writes=[bhT])

    baoraw = [Buf() for _ in range(JTA)]
    baotmp = Buf()

    def epi_quick(pOb, j_slot):
        B.op("s", lambda s, pOb=pOb, j_slot=j_slot: s.copy(sb["aoraw"][:, j_slot, :], pO[pOb][:, :]),
             reads=[bpO[pOb]], writes=[baoraw[j_slot]])

    def epi_batch(cb, jt, bias_name, bias_buf, store_dst, g0, sq_on_vector):
        xs = sb["xsb%d" % cb]
        ao = sb["aoraw"]
        for j in range(jt):
            B.op("v", lambda v, j=j, ao=ao: v.scalar_tensor_tensor(
                ao[:, j, :], ao[:, j, :], 1.0, sb[bias_name][:, :],
                ALU.mult, ALU.add, accum_out=stc("s2", j)),
                reads=[baoraw[j], bias_buf], writes=[baoraw[j], bst["s2"]])
        for j in range(jt):
            if sq_on_vector:
                B.op("v", lambda v, j=j, ao=ao: v.scalar_tensor_tensor(
                    sb["trash"][:, :], ao[:, j, :], 1.0, ao[:, j, :],
                    ALU.mult, ALU.mult, accum_out=stc("q2", j)),
                    reads=[baoraw[j]], writes=[btrash, bst["q2"]])
            else:
                B.op("s", lambda s, j=j, ao=ao: s.activation(
                    sb["trash"][:, :], ao[:, j, :], AF.Square,
                    bias=0.0, scale=1.0, accum_out=stc("q2", j)),
                    reads=[baoraw[j]], writes=[btrash, bst["q2"]])
        B.op("v", lambda v: v.tensor_scalar_mul(stc("m2", 0, jt), stc("s2", 0, jt), 1.0 / D),
             reads=[bst["s2"]], writes=[bst["m2"]])
        B.op("v", lambda v: v.tensor_scalar_mul(stc("e2", 0, jt), stc("q2", 0, jt), 1.0 / D),
             reads=[bst["q2"]], writes=[bst["e2"]])
        B.op("v", lambda v: v.tensor_mul(stc("ms2", 0, jt), stc("m2", 0, jt), stc("m2", 0, jt)),
             reads=[bst["m2"]], writes=[bst["ms2"]])
        B.op("v", lambda v: v.scalar_tensor_tensor(stc("v2", 0, jt), stc("e2", 0, jt), EPS,
                                                   stc("ms2", 0, jt), ALU.add, ALU.subtract),
             reads=[bst["e2"], bst["ms2"]], writes=[bst["v2"]])
        B.op("s", lambda s: s.activation(stc("sd2", 0, jt), stc("v2", 0, jt), AF.Sqrt,
                                         bias=0.0, scale=1.0),
             reads=[bst["v2"]], writes=[bst["sd2"]])
        B.op("v", lambda v: v.reciprocal(stc("r2", 0, jt), stc("sd2", 0, jt)),
             reads=[bst["sd2"]], writes=[bst["r2"]])
        for j in range(jt):
            B.op("v", lambda v, j=j, ao=ao: v.tensor_scalar(
                ao[:, j, :], ao[:, j, :], stc("m2", j), stc("r2", j), ALU.subtract, ALU.mult),
                reads=[baoraw[j], bst["m2"], bst["r2"]], writes=[baoraw[j]])
            B.op("v", lambda v, j=j, xs=xs, ao=ao: v.tensor_add(xs[:, j, :], xs[:, j, :], ao[:, j, :]),
                 reads=[xt[cb][j], baoraw[j]], writes=[xt[cb][j]])
            g = g0 + j
            extra = [bx1s[g]] if store_dst is x1s_h else []
            if j % 2 == 0:
                B.op("sy", lambda s, g=g, j=j, xs=xs: s.dma_start(store_dst[g], xs[:, j, :]),
                     reads=[xt[cb][j]], writes=extra, queue="qx")
            else:
                B.op("s", lambda s, g=g, j=j, xs=xs: s.dma_start(store_dst[g], xs[:, j, :]),
                     reads=[xt[cb][j]], writes=extra, queue="qo")

    for c in range(NCHA):
        cb = c % 2
        ln_batched(cb, xt[cb], JTA)
        transposes(cb, JTA)
        for dt in range(KT):
            pb = dt % 2
            for k in range(KT):
                B.op("t", lambda t, pb=pb, k=k, dt=dt: t.matmul(
                    pV[pb][:, 0:CHA], sb["wvT"][:, k, dt * 128:(dt + 1) * 128], sb["hT"][:, k, :],
                    start=(k == 0), stop=(k == KT - 1)),
                    reads=[bwv[dt // 4], bhT], writes=[bpV[pb]])
            B.op("s", lambda s, pb=pb, dt=dt: s.activation(
                sb["vT"][:, dt, :], pV[pb][:, 0:CHA], AF.Identity,
                bias=sb["bv"][:, dt:dt + 1], scale=1.0),
                reads=[bpV[pb], bbv], writes=[vTb[dt]])
        if c >= 1:
            epi_batch((c - 1) % 2, JTA, "boutb", bboutb, x1s_h, JTA * (c - 1), sq_on_vector=False)
        if c + 1 < NCHA:
            load_x(xh, c + 1, JTA)
        for j in range(JTA):
            for nch in range(2):
                for k in range(KT):
                    B.op("t", lambda t, j=j, nch=nch, k=k: t.matmul(
                        pO[j % 2][:, nch * 512:(nch + 1) * 512],
                        sb["vT"][:, k, j * 128:(j + 1) * 128],
                        sb["woutT"][:, k, nch * 512:(nch + 1) * 512],
                        start=(k == 0), stop=(k == KT - 1)),
                        reads=[vTb[k], bwo], writes=[bpO[j % 2]])
            epi_quick(j % 2, j)
    epi_batch((NCHA - 1) % 2, JTA, "boutb", bboutb, x1s_h, JTA * (NCHA - 1), sq_on_vector=False)

    B.op("g", lambda g: g.dma_start(sb["woThi"][:, 0:8, :], wot_h[:, 16:24, :]),
         writes=[bwoT[2]] + bwv, queue="qw")
    B.op("g", lambda g: g.dma_start(sb["woThi"][:, 8:16, :], wot_h[:, 24:32, :]),
         writes=[bwoT[3], bwo], queue="qw")
    B.op("g", lambda g: g.dma_start(sb["b2a"][:, :], b2_h[:, 0:FF // 2]),
         writes=[bb2[0]] + vTb, queue="qw")
    B.op("g", lambda g: g.dma_start(sb["b2b"][:, :], b2_h[:, FF // 2:FF]),
         writes=[bb2[1], bboutb] + vTb, queue="qw")
    B.op("sy", lambda s: s.dma_start(sb["wobb"][:, :], wobb_h[:, :]),
         writes=[bwobb] + vTb, queue="qx")

    load_x(x1s_h, 0, JT)
    for c in range(NCH):
        cb = c % 2
        ln_batched(cb, xt[cb], JT)
        transposes(cb, JT)
        for k in range(KT):
            B.op("t", lambda t, k=k: t.matmul(pV[0][0:8, 0:CH], sb["gwt"][:, k, :], sb["hT"][:, k, 0:CH],
                                              start=(k == 0), stop=(k == KT - 1)),
                 reads=[bgwt, bhT], writes=[bpV[0]])
        for k in range(KT):
            B.op("t", lambda t, k=k: t.matmul(pV[0][0:32, 256:256 + CH], sb["a2t"][:, k, :],
                                              sb["hT"][:, k, 0:CH], start=(k == 0), stop=(k == KT - 1)),
                 reads=[ba2t, bhT], writes=[bpV[0]])
        B.op("s", lambda s: s.activation(sb["lg"][:, :], pV[0][0:8, 0:CH], AF.Identity,
                                         bias=sb["gateb"][:, 0:1], scale=1.0),
             reads=[bpV[0], bgb], writes=[blg])
        for j in range(JT):
            B.op("t", lambda t, j=j: t.transpose(pT[1][:, j * 8:(j + 1) * 8],
                                                 sb["lg"][:, j * 128:(j + 1) * 128],
                                                 sb["eyebf"][0:8, 0:8]),
                 reads=[blg, beye], writes=[bpT[1]])
            B.op("s", lambda s, j=j: s.copy(sb["lgT"][:, j * 8:(j + 1) * 8], pT[1][:, j * 8:(j + 1) * 8]),
                 reads=[bpT[1]], writes=[blgT])
            B.op("v", lambda v, j=j: v.reduce_max(stc("mx", j), sb["lgT"][:, j * 8:(j + 1) * 8], axis=AX.X),
                 reads=[blgT], writes=[bst["mx"]])
            for r in range(4):
                B.op("v", lambda v, j=j, r=r: v.tensor_scalar(
                    sb["maskT"][:, j, r * 8:(r + 1) * 8], sb["lgT"][:, j * 8:(j + 1) * 8],
                    stc("mx", j), None, ALU.is_ge),
                    reads=[blgT, bst["mx"]], writes=[bmaskT[j]])
            B.op("t", lambda t, j=j: t.transpose(pT[1][0:32, 128 + j * 128:256 + j * 128],
                                                 sb["maskT"][:, j, :], sb["eyebf"][:, :]),
                 reads=[bmaskT[j], beye], writes=[bpT[1]])
            B.op("s", lambda s, j=j: s.copy(sb["mask32"][0:32, j * 128:(j + 1) * 128],
                                            pT[1][0:32, 128 + j * 128:256 + j * 128]),
                 reads=[bpT[1]], writes=[bmask32])
        B.op("v", lambda v: v.tensor_mul(sb["xam"][0:32, :], pV[0][0:32, 256:256 + CH],
                                         sb["mask32"][0:32, :]),
             reads=[bpV[0], bmask32], writes=[bxam])
        def wo_emit(ft):
            ib = ft % 2
            wo_sb = sb["woTlo"] if ft < 16 else sb["woThi"]
            fl = ft if ft < 16 else ft - 16
            for j in range(JT):
                for nch in range(2):
                    B.op("t", lambda t, ib=ib, j=j, nch=nch, wo_sb=wo_sb, fl=fl, ft=ft: t.matmul(
                        pO[j][:, nch * 512:(nch + 1) * 512],
                        sb["if%d" % ib][:, j * 128:(j + 1) * 128],
                        wo_sb[:, fl, nch * 512:(nch + 1) * 512],
                        start=(ft == 0), stop=(ft == FT - 1)),
                        reads=[bif[ib], bwoT[ft // 8]], writes=[bpO[j]])

        for ft in range(FT):
            if ft == 8:
                if c >= 1:
                    epi_batch((c - 1) % 2, JT, "wobb", bwobb, yh, JT * (c - 1), sq_on_vector=True)
                if c + 1 < NCH:
                    load_x(x1s_h, c + 1, JT)
            ib = ft % 2
            pw = pWI[ib]
            bpw = bpWI[ib]
            for k in range(KT):
                B.op("t", lambda t, k=k, ft=ft, pw=pw: t.matmul(
                    pw[:, 0:CH], sb["wiT"][:, k, ft * 128:(ft + 1) * 128], sb["hT"][:, k, 0:CH],
                    start=(k == 0), stop=False),
                    reads=[bwiT[k // 2], bhT], writes=[bpw])
            b2t = sb["b2a"] if ft < 16 else sb["b2b"]
            fb = ft if ft < 16 else ft - 16
            B.op("t", lambda t, pw=pw, b2t=b2t, fb=fb: t.matmul(pw[:, 0:CH], b2t[0:32, fb * 128:(fb + 1) * 128],
                                                       sb["xam"][0:32, :], start=False, stop=True),
                 reads=[bb2[0 if ft < 16 else 1], bxam], writes=[bpw])
            B.op("s", lambda s, ib=ib, ft=ft, pw=pw: s.activation(
                sb["if%d" % ib][:, :], pw[:, 0:CH], AF.Relu,
                bias=sb["wib"][:, ft:ft + 1], scale=1.0),
                reads=[bpw, bwib], writes=[bif[ib]])
            if ft >= 1:
                wo_emit(ft - 1)
        wo_emit(FT - 1)
        for j in range(JT):
            epi_quick(j, j)
    epi_batch((NCH - 1) % 2, JT, "wobb", bwobb, yh, JT * (NCH - 1), sq_on_vector=True)

    B.raw_wait("sy", "qx", 16 * B.n["qx"])
    B.raw_wait("s", "qo", 16 * B.n["qo"])

    with (
        nc.semaphore("sem_t") as sem_t,
        nc.semaphore("sem_v") as sem_v,
        nc.semaphore("sem_s") as sem_s,
        nc.semaphore("sem_qx") as sem_qx,
        nc.semaphore("sem_qw") as sem_qw,
        nc.semaphore("sem_qo") as sem_qo,
        nc.Block() as block,
    ):
        sems = {"t": sem_t, "v": sem_v, "s": sem_s, "qx": sem_qx, "qw": sem_qw, "qo": sem_qo}

        def replay(eng_key):
            def run(h):
                for item in B.ops[eng_key]:
                    if item[0] == "wait":
                        h.wait_ge(sems[item[1]], item[2])
                    else:
                        _, fn, semkey, inc = item
                        fn(h).then_inc(sems[semkey], inc)
            return run

        @block.tensor
        def _(t):
            replay("t")(t)

        @block.vector
        def _(v):
            replay("v")(v)

        @block.scalar
        def _(s):
            replay("s")(s)

        @block.sync
        def _(sy):
            replay("sy")(sy)

        @block.gpsimd
        def _(g):
            replay("g")(g)

    return nc


def _prep(inp):
    w_in = inp["L0_attn_w_in"]
    b_in = inp["L0_attn_b_in"]
    w_out = inp["L0_attn_w_out"]
    b_out = inp["L0_attn_b_out"]
    gate_W = inp["L0_ffn_gate_W"]
    gate_b = inp["L0_ffn_gate_b"]
    wi_W = inp["L0_ffn_wi_W"]
    wi_b = inp["L0_ffn_wi_b"]
    wo_W = inp["L0_ffn_wo_W"]
    wo_b = inp["L0_ffn_wo_b"]
    lora_A = inp["L0_ffn_lora_A"]
    lora_B = inp["L0_ffn_lora_B"]

    A = np.ascontiguousarray
    wv = w_in.reshape(16, 3, 64, D)[:, 2].reshape(D, D)
    d = {
        "wvt": A(wv.T.reshape(KT, 128, D).transpose(1, 0, 2)),
        "wout": A(w_out.T.reshape(KT, 128, D).transpose(1, 0, 2)),
        "bvp": A(b_in.reshape(16, 3, 64)[:, 2].reshape(KT, 128).T),
        "boutb": A(np.broadcast_to(b_out, (128, D))),
        "gwt": A(gate_W.T.reshape(KT, 128, 8).transpose(1, 0, 2)),
        "gb": A(gate_b.reshape(8, 1)),
        "wit": A(wi_W.T.reshape(KT, 128, FF).transpose(1, 0, 2)),
        "wibp": A(wi_b.reshape(FT, 128).T),
        "wot": A(wo_W.T.reshape(FT, 128, D).transpose(1, 0, 2)),
        "wobb": A(np.broadcast_to(wo_b, (128, D))),
        "a2t": A(lora_A.transpose(1, 0, 2).reshape(32, D).T.reshape(KT, 128, 32).transpose(1, 0, 2)),
        "b2": A(lora_B.transpose(2, 0, 1).reshape(32, FF)),
        "eye": np.eye(128, dtype=np.float32),
    }
    return {k: np.asarray(v, dtype=np.float32) for k, v in d.items()}


def kernel(**inputs):
    inputs = {k: np.asarray(v) for k, v in inputs.items()}
    x = np.ascontiguousarray(inputs["L0_attn_x"], dtype=np.float32)
    xf = x.reshape(8 * 2048, D)
    w = _prep(inputs)
    nc = build_nc()
    in_maps = []
    for c in range(8):
        m = dict(w)
        m["x"] = np.ascontiguousarray(xf[c * TPC:(c + 1) * TPC].reshape(NT, 128, D))
        in_maps.append(m)
    res = run_bass_kernel_spmd(nc, in_maps, core_ids=list(range(8)))
    out = np.concatenate([r["y"].reshape(TPC, D) for r in res.results], axis=0)
    return out.reshape(8, 2048, D).astype(np.float32)


if __name__ == "__main__":
    rng = np.random.default_rng(0)
    fake = {
        "L0_attn_x": rng.standard_normal((8, 2048, 1024), dtype=np.float32),
        "L0_attn_w_in": rng.standard_normal((3072, 1024), dtype=np.float32) * 0.02,
        "L0_attn_b_in": rng.standard_normal((3072,), dtype=np.float32) * 0.02,
        "L0_attn_w_out": rng.standard_normal((1024, 1024), dtype=np.float32) * 0.02,
        "L0_attn_b_out": rng.standard_normal((1024,), dtype=np.float32) * 0.02,
        "L0_ffn_gate_W": rng.standard_normal((8, 1024), dtype=np.float32) * 0.03,
        "L0_ffn_gate_b": rng.standard_normal((8,), dtype=np.float32) * 0.01,
        "L0_ffn_wi_W": rng.standard_normal((4096, 1024), dtype=np.float32) * 0.03,
        "L0_ffn_wi_b": rng.standard_normal((4096,), dtype=np.float32) * 0.01,
        "L0_ffn_wo_W": rng.standard_normal((1024, 4096), dtype=np.float32) * 0.015,
        "L0_ffn_wo_b": rng.standard_normal((1024,), dtype=np.float32) * 0.01,
        "L0_ffn_lora_A": rng.standard_normal((8, 4, 1024), dtype=np.float32) * 0.03,
        "L0_ffn_lora_B": rng.standard_normal((8, 4096, 4), dtype=np.float32) * 0.02,
    }
    out = kernel(**fake)
    print("out", out.shape, out.dtype, float(np.abs(out).mean()))


# version 12 (final)
# speedup vs baseline: 1.2828x; 1.0159x over previous
import numpy as np
import concourse.bass as bass
import concourse.mybir as mybir
from concourse.bass_utils import run_bass_kernel_spmd

F32 = mybir.dt.float32
BF16 = mybir.dt.bfloat16
AF = mybir.ActivationFunctionType
ALU = mybir.AluOpType
AX = mybir.AxisListType

D = 1024
FF = 4096
KT = 8
FT = 32
TPC = 2048
CH = 256
NCH = TPC // CH
JT = CH // 128
CHA = 512
NCHA = TPC // CHA
JTA = CHA // 128
NT = TPC // 128
EPS = 1e-5


class Buf:
    __slots__ = ("w", "r")

    def __init__(self):
        self.w = None
        self.r = []


class Bld:

    ENG_SECT = {"t": "tensor", "v": "vector", "s": "scalar", "sy": "sync", "g": "gpsimd"}

    def __init__(self):
        self.ops = {e: [] for e in self.ENG_SECT}
        self.n = {k: 0 for k in ("t", "v", "s", "qx", "qw", "qo")}
        self.waited = {}
        self.dummy_fn = {}
        self.dmy_buf = {"v": Buf(), "s": Buf()}

    def _wait(self, eng, dep):
        stream, val = dep
        if stream == eng:
            if eng == "t":
                return
        key = (eng, stream)
        if self.waited.get(key, 0) >= val:
            return
        self.waited[key] = val
        self.ops[eng].append(("wait", stream, val))

    def op(self, eng, fn, reads=(), writes=(), queue=None):
        if eng in ("v", "s") and self.dummy_fn.get(eng) is not None:
            last = (eng, self.n[eng])
            if any(b.w == last for b in reads):
                self.op(eng, self.dummy_fn[eng], reads=(), writes=(self.dmy_buf[eng],))
        for b in reads:
            if b.w is not None:
                self._wait(eng, b.w)
        for b in writes:
            if b.w is not None:
                self._wait(eng, b.w)
            for rd in b.r:
                self._wait(eng, rd)
        if queue is None:
            self.n[eng] += 1
            tok = (eng, self.n[eng])
            self.ops[eng].append(("do", fn, eng, 1))
        else:
            if self.n[queue] > 0:
                self._wait(eng, (queue, 16 * self.n[queue]))
            self.n[queue] += 1
            tok = (queue, 16 * self.n[queue])
            self.ops[eng].append(("do", fn, queue, 16))
        for b in reads:
            b.r.append(tok)
        for b in writes:
            b.w = tok
            b.r = []

    def raw_wait(self, eng, stream, val):
        key = (eng, stream)
        if self.waited.get(key, 0) >= val:
            return
        self.waited[key] = val
        self.ops[eng].append(("wait", stream, val))


def build_nc():
    nc = bass.Bass("TRN2", target_bir_lowering=False)

    xh = nc.dram_tensor("x", [NT, 128, D], F32, kind="ExternalInput")
    wvt_h = nc.dram_tensor("wvt", [128, KT, D], F32, kind="ExternalInput")
    wout_h = nc.dram_tensor("wout", [128, KT, D], F32, kind="ExternalInput")
    bv_h = nc.dram_tensor("bvp", [128, KT], F32, kind="ExternalInput")
    boutb_h = nc.dram_tensor("boutb", [128, D], F32, kind="ExternalInput")
    gwt_h = nc.dram_tensor("gwt", [128, KT, 8], F32, kind="ExternalInput")
    gb_h = nc.dram_tensor("gb", [8, 1], F32, kind="ExternalInput")
    wit_h = nc.dram_tensor("wit", [128, KT, FF], F32, kind="ExternalInput")
    wib_h = nc.dram_tensor("wibp", [128, FT], F32, kind="ExternalInput")
    wot_h = nc.dram_tensor("wot", [128, FT, D], F32, kind="ExternalInput")
    wobb_h = nc.dram_tensor("wobb", [128, D], F32, kind="ExternalInput")
    a2t_h = nc.dram_tensor("a2t", [128, KT, 32], F32, kind="ExternalInput")
    b2_h = nc.dram_tensor("b2", [32, FF], F32, kind="ExternalInput")
    eye_h = nc.dram_tensor("eye", [128, 128], F32, kind="ExternalInput")
    x1s_h = nc.dram_tensor("x1s", [NT, 128, D], F32, kind="Internal")
    yh = nc.dram_tensor("y", [NT, 128, D], F32, kind="ExternalOutput")

    layout = []

    def L(name, shape, dtype):
        layout.append((name, shape, dtype))
        return name

    static = [
        ("xsb0", [128, JTA, D], F32), ("xsb1", [128, JTA, D], F32),
        ("htmp", [128, 2, D], BF16),
        ("hT", [128, KT, CHA], BF16),
        ("trash", [128, D], BF16),
        ("eyebf", [128, 128], BF16),
        ("stats", [128, 80], F32),
        ("lg", [8, CH], BF16),
        ("lgT", [128, 16], BF16),
        ("maskT", [128, JT, 32], BF16),
        ("mask32", [32, CH], F32),
        ("xam", [32, 2 * CH], BF16),
        ("bv", [128, KT], F32),
        ("gateb", [8, 16], F32),
        ("wib", [128, FT], F32),
        ("a2t", [128, KT, 32], BF16),
        ("gwt", [128, KT, 8], BF16),
        ("if0", [128, CH], BF16), ("if1", [128, CH], BF16),
        ("dmyv", [128, 64], F32), ("dmys", [128, 64], F32),
        ("aoraw", [128, JTA, D], F32),
        ("wiT", [128, KT, FF], BF16),
        ("woTlo", [128, 16, D], BF16),
    ]
    dt_size = {F32: 4, BF16: 2}
    off = 0
    offs = {}
    for name, shape, dtype in static:
        per_part = int(np.prod(shape[1:])) * dt_size[dtype]
        offs[name] = off
        off += (per_part + 31) // 32 * 32
    alias_base = off
    alias = [
        ("wvT", [128, KT, D], BF16, 0),
        ("woutT", [128, KT, D], BF16, 16384),
        ("vT", [128, KT, CHA], BF16, 32768),
        ("boutb", [128, D], F32, 40960),
        ("woThi", [128, 16, D], BF16, 0),
        ("wobb", [128, D], F32, 32768),
        ("b2a", [32, FF // 2], BF16, 36864),
        ("b2b", [32, FF // 2], BF16, 40960),
    ]
    total = alias_base + 45056
    arena = nc.alloc_sbuf_tensor("arena", [128, total], mybir.dt.uint8)
    base = nc.lookup_mloc(arena).addr

    sb = {}
    for name, shape, dtype in static:
        sb[name] = nc.alloc_sbuf_tensor_at(name, shape, dtype, offset=base + offs[name])
    for name, shape, dtype, aoff in alias:
        sb[name] = nc.alloc_sbuf_tensor_at(name, shape, dtype, offset=base + alias_base + aoff)

    pT0 = nc.alloc_psum_tensor("pT0", [128, 1024], BF16)
    pV0 = nc.alloc_psum_tensor("pV0", [128, 512], F32)
    pV1 = nc.alloc_psum_tensor("pV1", [128, 512], F32)
    pW2 = nc.alloc_psum_tensor("pW2", [128, 512], F32)
    pO0 = nc.alloc_psum_tensor("pO0", [128, 1024], F32)
    pO1 = nc.alloc_psum_tensor("pO1", [128, 1024], F32)
    pT = [pT0, pT0]
    pV = [pV0, pV1]
    pWI = [pV1, pW2]
    pO = [pO0, pO1]

    B = Bld()
    B.dummy_fn["v"] = lambda v: v.tensor_copy(sb["dmyv"][:, 0:32], sb["dmyv"][:, 32:64])
    B.dummy_fn["s"] = lambda s: s.copy(sb["dmys"][:, 0:32], sb["dmys"][:, 32:64])
    xt = [[Buf() for _ in range(JTA)] for _ in range(2)]
    bhtmp = [Buf(), Buf()]
    bhT2 = [Buf(), Buf()]
    vTb = [Buf() for _ in range(KT)]
    _bpT0 = Buf()
    bpT = [_bpT0, _bpT0]
    bpV = [Buf(), Buf()]
    bpWI = [bpV[1], Buf()]
    bpO = [Buf(), Buf()]
    bwv = [Buf(), Buf()]; bwo = Buf(); bgwt = Buf(); ba2t = Buf(); beye = Buf()
    bb2 = [Buf(), Buf()]
    bwiT = [Buf() for _ in range(4)]
    bwoT = [Buf() for _ in range(4)]
    bbv = Buf(); bboutb = Buf(); bgb = Buf(); bwib = Buf(); bwobb = Buf()
    bst = {k: Buf() for k in ("s", "q", "m", "e", "ms", "v", "sd", "r",
                              "s2", "q2", "m2", "e2", "ms2", "v2", "sd2", "r2", "mx")}
    blg = Buf(); blgT = Buf(); bmaskT = [Buf() for _ in range(JT)]
    bmask32 = Buf(); bxam2 = [Buf(), Buf()]
    bif = [Buf(), Buf()]
    bx1s = [Buf() for _ in range(NT)]
    btrash = Buf()

    SC = {"s": 0, "q": 4, "m": 8, "e": 12, "ms": 16, "v": 20, "sd": 24, "r": 28,
          "s2": 32, "q2": 36, "m2": 40, "e2": 44, "ms2": 48, "v2": 52, "sd2": 56,
          "r2": 60, "mx": 64}

    st = sb["stats"]

    def stc(key, j=0, n=1):
        c = SC[key] + j
        return st[:, c:c + n]

    B.op("v", lambda v: v.memset(sb["dmyv"][:, :], 0.0), writes=[B.dmy_buf["v"]])
    B.op("s", lambda s: s.memzero(sb["dmys"][:, :]), writes=[B.dmy_buf["s"]])

    B.op("g", lambda g: g.dma_start(sb["eyebf"][:, :], eye_h[:, :]), writes=[beye], queue="qw")
    B.op("g", lambda g: g.dma_start(sb["wvT"][:, :, 0:512], wvt_h[:, :, 0:512]),
         writes=[bwv[0]], queue="qw")
    B.op("g", lambda g: g.dma_start(sb["wvT"][:, :, 512:1024], wvt_h[:, :, 512:1024]),
         writes=[bwv[1]], queue="qw")
    B.op("g", lambda g: g.dma_start(sb["woutT"][:, :, :], wout_h[:, :, :]), writes=[bwo], queue="qw")
    B.op("g", lambda g: g.dma_start(sb["gwt"][:, :, :], gwt_h[:, :, :]), writes=[bgwt], queue="qw")
    B.op("g", lambda g: g.dma_start(sb["a2t"][:, :, :], a2t_h[:, :, :]), writes=[ba2t], queue="qw")
    for gidx in range(4):
        k0 = 2 * gidx
        B.op("g", lambda g, k0=k0: g.dma_start(sb["wiT"][:, k0:k0 + 2, :], wit_h[:, k0:k0 + 2, :]),
             writes=[bwiT[gidx]], queue="qw")
    for gidx in range(2):
        f0 = 8 * gidx
        B.op("g", lambda g, f0=f0: g.dma_start(sb["woTlo"][:, f0:f0 + 8, :], wot_h[:, f0:f0 + 8, :]),
             writes=[bwoT[gidx]], queue="qw")
    def load_x(src, c, jt):
        cb = c % 2
        for j in range(jt):
            g = jt * c + j
            eng, q = ("sy", "qx") if j % 2 == 0 else ("s", "qo")
            B.op(eng, lambda s, g=g, cb=cb, j=j: s.dma_start(sb["xsb%d" % cb][:, j, :], src[g]),
                 reads=([bx1s[g]] if src is x1s_h else []), writes=[xt[cb][j]], queue=q)

    load_x(xh, 0, JTA)
    B.op("sy", lambda s: s.dma_start(sb["bv"][:, :], bv_h[:, :]), writes=[bbv], queue="qx")
    B.op("sy", lambda s: s.dma_start(sb["boutb"][:, :], boutb_h[:, :]), writes=[bboutb], queue="qx")
    B.op("sy", lambda s: s.dma_start(sb["gateb"][:, 0:1], gb_h[:, :]), writes=[bgb], queue="qx")
    B.op("sy", lambda s: s.dma_start(sb["wib"][:, :], wib_h[:, :]), writes=[bwib], queue="qx")

    def ln_batched(cb, inp_bufs, jt):
        xs = sb["xsb%d" % cb]
        hm = sb["htmp"]
        for j in range(jt):
            B.op("v", lambda v, j=j, xs=xs: v.reduce_sum(stc("s", j), xs[:, j, :], axis=AX.X),
                 reads=[inp_bufs[j]], writes=[bst["s"]])
        for j in range(jt):
            B.op("s", lambda s, j=j, xs=xs: s.activation(sb["trash"][:, :], xs[:, j, :], AF.Square,
                                                         bias=0.0, scale=1.0, accum_out=stc("q", j)),
                 reads=[inp_bufs[j]], writes=[btrash, bst["q"]])
        B.op("v", lambda v: v.tensor_scalar_mul(stc("m", 0, jt), stc("s", 0, jt), 1.0 / D),
             reads=[bst["s"]], writes=[bst["m"]])
        B.op("v", lambda v: v.tensor_scalar_mul(stc("e", 0, jt), stc("q", 0, jt), 1.0 / D),
             reads=[bst["q"]], writes=[bst["e"]])
        B.op("v", lambda v: v.tensor_mul(stc("ms", 0, jt), stc("m", 0, jt), stc("m", 0, jt)),
             reads=[bst["m"]], writes=[bst["ms"]])
        B.op("v", lambda v: v.scalar_tensor_tensor(stc("v", 0, jt), stc("e", 0, jt), EPS,
                                                   stc("ms", 0, jt), ALU.add, ALU.subtract),
             reads=[bst["e"], bst["ms"]], writes=[bst["v"]])
        B.op("s", lambda s: s.activation(stc("sd", 0, jt), stc("v", 0, jt), AF.Sqrt, bias=0.0, scale=1.0),
             reads=[bst["v"]], writes=[bst["sd"]])
        B.op("v", lambda v: v.reciprocal(stc("r", 0, jt), stc("sd", 0, jt)),
             reads=[bst["sd"]], writes=[bst["r"]])

    def transposes(cb, jt, hto, hbufs):
        xs = sb["xsb%d" % cb]
        hm = sb["htmp"]
        for j in range(jt):
            B.op("v", lambda v, j=j, xs=xs, hm=hm: v.tensor_scalar(
                hm[:, j % 2, :], xs[:, j, :], stc("m", j), stc("r", j), ALU.subtract, ALU.mult),
                reads=[xt[cb][j], bst["m"], bst["r"]], writes=[bhtmp[j % 2]])
            for dt in range(KT):
                B.op("t", lambda t, j=j, dt=dt, hm=hm: t.transpose(
                    pT[0][:, dt * 128:(dt + 1) * 128],
                    hm[:, j % 2, dt * 128:(dt + 1) * 128], sb["eyebf"][:, :]),
                    reads=[bhtmp[j % 2], beye], writes=[bpT[0]])
            B.op("s", lambda s, j=j, hto=hto: s.copy(
                sb["hT"][:, :, hto + j * 128: hto + (j + 1) * 128],
                pT[0][:, :].rearrange("p (a b) -> p a b", b=128)),
                reads=[bpT[0]], writes=hbufs)

    baoraw = [Buf() for _ in range(JTA)]
    baotmp = Buf()

    def epi_quick(pOb, j_slot):
        B.op("s", lambda s, pOb=pOb, j_slot=j_slot: s.copy(sb["aoraw"][:, j_slot, :], pO[pOb][:, :]),
             reads=[bpO[pOb]], writes=[baoraw[j_slot]])

    def epi_batch(cb, jt, bias_name, bias_buf, store_dst, g0, sq_on_vector):
        xs = sb["xsb%d" % cb]
        ao = sb["aoraw"]
        for j in range(jt):
            B.op("v", lambda v, j=j, ao=ao: v.scalar_tensor_tensor(
                ao[:, j, :], ao[:, j, :], 1.0, sb[bias_name][:, :],
                ALU.mult, ALU.add, accum_out=stc("s2", j)),
                reads=[baoraw[j], bias_buf], writes=[baoraw[j], bst["s2"]])
        for j in range(jt):
            if sq_on_vector:
                B.op("v", lambda v, j=j, ao=ao: v.scalar_tensor_tensor(
                    sb["trash"][:, :], ao[:, j, :], 1.0, ao[:, j, :],
                    ALU.mult, ALU.mult, accum_out=stc("q2", j)),
                    reads=[baoraw[j]], writes=[btrash, bst["q2"]])
            else:
                B.op("s", lambda s, j=j, ao=ao: s.activation(
                    sb["trash"][:, :], ao[:, j, :], AF.Square,
                    bias=0.0, scale=1.0, accum_out=stc("q2", j)),
                    reads=[baoraw[j]], writes=[btrash, bst["q2"]])
        B.op("v", lambda v: v.tensor_scalar_mul(stc("m2", 0, jt), stc("s2", 0, jt), 1.0 / D),
             reads=[bst["s2"]], writes=[bst["m2"]])
        B.op("v", lambda v: v.tensor_scalar_mul(stc("e2", 0, jt), stc("q2", 0, jt), 1.0 / D),
             reads=[bst["q2"]], writes=[bst["e2"]])
        B.op("v", lambda v: v.tensor_mul(stc("ms2", 0, jt), stc("m2", 0, jt), stc("m2", 0, jt)),
             reads=[bst["m2"]], writes=[bst["ms2"]])
        B.op("v", lambda v: v.scalar_tensor_tensor(stc("v2", 0, jt), stc("e2", 0, jt), EPS,
                                                   stc("ms2", 0, jt), ALU.add, ALU.subtract),
             reads=[bst["e2"], bst["ms2"]], writes=[bst["v2"]])
        B.op("s", lambda s: s.activation(stc("sd2", 0, jt), stc("v2", 0, jt), AF.Sqrt,
                                         bias=0.0, scale=1.0),
             reads=[bst["v2"]], writes=[bst["sd2"]])
        B.op("v", lambda v: v.reciprocal(stc("r2", 0, jt), stc("sd2", 0, jt)),
             reads=[bst["sd2"]], writes=[bst["r2"]])
        for j in range(jt):
            B.op("v", lambda v, j=j, ao=ao: v.tensor_scalar(
                ao[:, j, :], ao[:, j, :], stc("m2", j), stc("r2", j), ALU.subtract, ALU.mult),
                reads=[baoraw[j], bst["m2"], bst["r2"]], writes=[baoraw[j]])
            B.op("v", lambda v, j=j, xs=xs, ao=ao: v.tensor_add(xs[:, j, :], xs[:, j, :], ao[:, j, :]),
                 reads=[xt[cb][j], baoraw[j]], writes=[xt[cb][j]])
            g = g0 + j
            extra = [bx1s[g]] if store_dst is x1s_h else []
            if j % 2 == 0:
                B.op("sy", lambda s, g=g, j=j, xs=xs: s.dma_start(store_dst[g], xs[:, j, :]),
                     reads=[xt[cb][j]], writes=extra, queue="qx")
            else:
                B.op("s", lambda s, g=g, j=j, xs=xs: s.dma_start(store_dst[g], xs[:, j, :]),
                     reads=[xt[cb][j]], writes=extra, queue="qo")

    for c in range(NCHA):
        cb = c % 2
        ln_batched(cb, xt[cb], JTA)
        transposes(cb, JTA, 0, bhT2)
        for dt in range(KT):
            pb = dt % 2
            for k in range(KT):
                B.op("t", lambda t, pb=pb, k=k, dt=dt: t.matmul(
                    pV[pb][:, 0:CHA], sb["wvT"][:, k, dt * 128:(dt + 1) * 128], sb["hT"][:, k, :],
                    start=(k == 0), stop=(k == KT - 1)),
                    reads=[bwv[dt // 4]] + bhT2, writes=[bpV[pb]])
            B.op("s", lambda s, pb=pb, dt=dt: s.activation(
                sb["vT"][:, dt, :], pV[pb][:, 0:CHA], AF.Identity,
                bias=sb["bv"][:, dt:dt + 1], scale=1.0),
                reads=[bpV[pb], bbv], writes=[vTb[dt]])
        if c >= 1:
            epi_batch((c - 1) % 2, JTA, "boutb", bboutb, x1s_h, JTA * (c - 1), sq_on_vector=False)
        if c + 1 < NCHA:
            load_x(xh, c + 1, JTA)
        for j in range(JTA):
            for nch in range(2):
                for k in range(KT):
                    B.op("t", lambda t, j=j, nch=nch, k=k: t.matmul(
                        pO[j % 2][:, nch * 512:(nch + 1) * 512],
                        sb["vT"][:, k, j * 128:(j + 1) * 128],
                        sb["woutT"][:, k, nch * 512:(nch + 1) * 512],
                        start=(k == 0), stop=(k == KT - 1)),
                        reads=[vTb[k], bwo], writes=[bpO[j % 2]])
            epi_quick(j % 2, j)
    epi_batch((NCHA - 1) % 2, JTA, "boutb", bboutb, x1s_h, JTA * (NCHA - 1), sq_on_vector=False)

    B.op("g", lambda g: g.dma_start(sb["woThi"][:, 0:8, :], wot_h[:, 16:24, :]),
         writes=[bwoT[2]] + bwv, queue="qw")
    B.op("g", lambda g: g.dma_start(sb["woThi"][:, 8:16, :], wot_h[:, 24:32, :]),
         writes=[bwoT[3], bwo], queue="qw")
    B.op("g", lambda g: g.dma_start(sb["b2a"][:, :], b2_h[:, 0:FF // 2]),
         writes=[bb2[0]] + vTb, queue="qw")
    B.op("g", lambda g: g.dma_start(sb["b2b"][:, :], b2_h[:, FF // 2:FF]),
         writes=[bb2[1], bboutb] + vTb, queue="qw")
    B.op("sy", lambda s: s.dma_start(sb["wobb"][:, :], wobb_h[:, :]),
         writes=[bwobb] + vTb, queue="qx")

    def b_prep(c):
        cb = c % 2
        hto = cb * CH
        hb = [bhT2[cb]]
        ln_batched(cb, xt[cb], JT)
        transposes(cb, JT, hto, hb)
        for k in range(KT):
            B.op("t", lambda t, k=k, hto=hto: t.matmul(
                pV[0][0:8, 0:CH], sb["gwt"][:, k, :], sb["hT"][:, k, hto:hto + CH],
                start=(k == 0), stop=(k == KT - 1)),
                reads=[bgwt] + hb, writes=[bpV[0]])
        for k in range(KT):
            B.op("t", lambda t, k=k, hto=hto: t.matmul(
                pV[0][0:32, 256:256 + CH], sb["a2t"][:, k, :],
                sb["hT"][:, k, hto:hto + CH], start=(k == 0), stop=(k == KT - 1)),
                reads=[ba2t] + hb, writes=[bpV[0]])
        B.op("s", lambda s: s.activation(sb["lg"][:, :], pV[0][0:8, 0:CH], AF.Identity,
                                         bias=sb["gateb"][:, 0:1], scale=1.0),
             reads=[bpV[0], bgb], writes=[blg])
        for j in range(JT):
            B.op("t", lambda t, j=j: t.transpose(pT[0][:, j * 8:(j + 1) * 8],
                                                 sb["lg"][:, j * 128:(j + 1) * 128],
                                                 sb["eyebf"][0:8, 0:8]),
                 reads=[blg, beye], writes=[bpT[0]])
            B.op("s", lambda s, j=j: s.copy(sb["lgT"][:, j * 8:(j + 1) * 8], pT[0][:, j * 8:(j + 1) * 8]),
                 reads=[bpT[0]], writes=[blgT])
            B.op("v", lambda v, j=j: v.reduce_max(stc("mx", j), sb["lgT"][:, j * 8:(j + 1) * 8], axis=AX.X),
                 reads=[blgT], writes=[bst["mx"]])
            for r in range(4):
                B.op("v", lambda v, j=j, r=r: v.tensor_scalar(
                    sb["maskT"][:, j, r * 8:(r + 1) * 8], sb["lgT"][:, j * 8:(j + 1) * 8],
                    stc("mx", j), None, ALU.is_ge),
                    reads=[blgT, bst["mx"]], writes=[bmaskT[j]])
            B.op("t", lambda t, j=j: t.transpose(pT[0][0:32, 128 + j * 128:256 + j * 128],
                                                 sb["maskT"][:, j, :], sb["eyebf"][:, :]),
                 reads=[bmaskT[j], beye], writes=[bpT[0]])
            B.op("s", lambda s, j=j: s.copy(sb["mask32"][0:32, j * 128:(j + 1) * 128],
                                            pT[0][0:32, 128 + j * 128:256 + j * 128]),
                 reads=[bpT[0]], writes=[bmask32])
        B.op("v", lambda v, hto=hto: v.tensor_mul(sb["xam"][0:32, hto:hto + CH],
                                                  pV[0][0:32, 256:256 + CH],
                                                  sb["mask32"][0:32, :]),
             reads=[bpV[0], bmask32], writes=[bxam2[cb]])

    load_x(x1s_h, 0, JT)
    b_prep(0)
    for c in range(NCH):
        cb = c % 2
        hto = cb * CH
        hb = [bhT2[cb]]
        xo = cb * CH

        def wo_emit(ft):
            ib = ft % 2
            wo_sb = sb["woTlo"] if ft < 16 else sb["woThi"]
            fl = ft if ft < 16 else ft - 16
            for j in range(JT):
                for nch in range(2):
                    B.op("t", lambda t, ib=ib, j=j, nch=nch, wo_sb=wo_sb, fl=fl, ft=ft: t.matmul(
                        pO[j][:, nch * 512:(nch + 1) * 512],
                        sb["if%d" % ib][:, j * 128:(j + 1) * 128],
                        wo_sb[:, fl, nch * 512:(nch + 1) * 512],
                        start=(ft == 0), stop=(ft == FT - 1)),
                        reads=[bif[ib], bwoT[ft // 8]], writes=[bpO[j]])

        for ft in range(FT):
            if ft == 8:
                if c >= 1:
                    epi_batch((c - 1) % 2, JT, "wobb", bwobb, yh, JT * (c - 1), sq_on_vector=True)
                if c + 1 < NCH:
                    load_x(x1s_h, c + 1, JT)
            if ft == 16 and c + 1 < NCH:
                b_prep(c + 1)
            ib = ft % 2
            pw = pWI[ib]
            bpw = bpWI[ib]
            for k in range(KT):
                B.op("t", lambda t, k=k, ft=ft, pw=pw, hto=hto: t.matmul(
                    pw[:, 0:CH], sb["wiT"][:, k, ft * 128:(ft + 1) * 128], sb["hT"][:, k, hto:hto + CH],
                    start=(k == 0), stop=False),
                    reads=[bwiT[k // 2]] + hb, writes=[bpw])
            b2t = sb["b2a"] if ft < 16 else sb["b2b"]
            fb = ft if ft < 16 else ft - 16
            B.op("t", lambda t, pw=pw, b2t=b2t, fb=fb, xo=xo: t.matmul(
                pw[:, 0:CH], b2t[0:32, fb * 128:(fb + 1) * 128],
                sb["xam"][0:32, xo:xo + CH], start=False, stop=True),
                 reads=[bb2[0 if ft < 16 else 1], bxam2[cb]], writes=[bpw])
            B.op("s", lambda s, ib=ib, ft=ft, pw=pw: s.activation(
                sb["if%d" % ib][:, :], pw[:, 0:CH], AF.Relu,
                bias=sb["wib"][:, ft:ft + 1], scale=1.0),
                reads=[bpw, bwib], writes=[bif[ib]])
            if ft >= 1:
                wo_emit(ft - 1)
        wo_emit(FT - 1)
        for j in range(JT):
            epi_quick(j, j)
    epi_batch((NCH - 1) % 2, JT, "wobb", bwobb, yh, JT * (NCH - 1), sq_on_vector=True)

    B.raw_wait("sy", "qx", 16 * B.n["qx"])
    B.raw_wait("s", "qo", 16 * B.n["qo"])

    with (
        nc.semaphore("sem_t") as sem_t,
        nc.semaphore("sem_v") as sem_v,
        nc.semaphore("sem_s") as sem_s,
        nc.semaphore("sem_qx") as sem_qx,
        nc.semaphore("sem_qw") as sem_qw,
        nc.semaphore("sem_qo") as sem_qo,
        nc.Block() as block,
    ):
        sems = {"t": sem_t, "v": sem_v, "s": sem_s, "qx": sem_qx, "qw": sem_qw, "qo": sem_qo}

        def replay(eng_key):
            def run(h):
                for item in B.ops[eng_key]:
                    if item[0] == "wait":
                        h.wait_ge(sems[item[1]], item[2])
                    else:
                        _, fn, semkey, inc = item
                        fn(h).then_inc(sems[semkey], inc)
            return run

        @block.tensor
        def _(t):
            replay("t")(t)

        @block.vector
        def _(v):
            replay("v")(v)

        @block.scalar
        def _(s):
            replay("s")(s)

        @block.sync
        def _(sy):
            replay("sy")(sy)

        @block.gpsimd
        def _(g):
            replay("g")(g)

    return nc


def _prep(inp):
    w_in = inp["L0_attn_w_in"]
    b_in = inp["L0_attn_b_in"]
    w_out = inp["L0_attn_w_out"]
    b_out = inp["L0_attn_b_out"]
    gate_W = inp["L0_ffn_gate_W"]
    gate_b = inp["L0_ffn_gate_b"]
    wi_W = inp["L0_ffn_wi_W"]
    wi_b = inp["L0_ffn_wi_b"]
    wo_W = inp["L0_ffn_wo_W"]
    wo_b = inp["L0_ffn_wo_b"]
    lora_A = inp["L0_ffn_lora_A"]
    lora_B = inp["L0_ffn_lora_B"]

    A = np.ascontiguousarray
    wv = w_in.reshape(16, 3, 64, D)[:, 2].reshape(D, D)
    d = {
        "wvt": A(wv.T.reshape(KT, 128, D).transpose(1, 0, 2)),
        "wout": A(w_out.T.reshape(KT, 128, D).transpose(1, 0, 2)),
        "bvp": A(b_in.reshape(16, 3, 64)[:, 2].reshape(KT, 128).T),
        "boutb": A(np.broadcast_to(b_out, (128, D))),
        "gwt": A(gate_W.T.reshape(KT, 128, 8).transpose(1, 0, 2)),
        "gb": A(gate_b.reshape(8, 1)),
        "wit": A(wi_W.T.reshape(KT, 128, FF).transpose(1, 0, 2)),
        "wibp": A(wi_b.reshape(FT, 128).T),
        "wot": A(wo_W.T.reshape(FT, 128, D).transpose(1, 0, 2)),
        "wobb": A(np.broadcast_to(wo_b, (128, D))),
        "a2t": A(lora_A.transpose(1, 0, 2).reshape(32, D).T.reshape(KT, 128, 32).transpose(1, 0, 2)),
        "b2": A(lora_B.transpose(2, 0, 1).reshape(32, FF)),
        "eye": np.eye(128, dtype=np.float32),
    }
    return {k: np.asarray(v, dtype=np.float32) for k, v in d.items()}


def kernel(**inputs):
    inputs = {k: np.asarray(v) for k, v in inputs.items()}
    x = np.ascontiguousarray(inputs["L0_attn_x"], dtype=np.float32)
    xf = x.reshape(8 * 2048, D)
    w = _prep(inputs)
    nc = build_nc()
    in_maps = []
    for c in range(8):
        m = dict(w)
        m["x"] = np.ascontiguousarray(xf[c * TPC:(c + 1) * TPC].reshape(NT, 128, D))
        in_maps.append(m)
    res = run_bass_kernel_spmd(nc, in_maps, core_ids=list(range(8)))
    out = np.concatenate([r["y"].reshape(TPC, D) for r in res.results], axis=0)
    return out.reshape(8, 2048, D).astype(np.float32)


if __name__ == "__main__":
    rng = np.random.default_rng(0)
    fake = {
        "L0_attn_x": rng.standard_normal((8, 2048, 1024), dtype=np.float32),
        "L0_attn_w_in": rng.standard_normal((3072, 1024), dtype=np.float32) * 0.02,
        "L0_attn_b_in": rng.standard_normal((3072,), dtype=np.float32) * 0.02,
        "L0_attn_w_out": rng.standard_normal((1024, 1024), dtype=np.float32) * 0.02,
        "L0_attn_b_out": rng.standard_normal((1024,), dtype=np.float32) * 0.02,
        "L0_ffn_gate_W": rng.standard_normal((8, 1024), dtype=np.float32) * 0.03,
        "L0_ffn_gate_b": rng.standard_normal((8,), dtype=np.float32) * 0.01,
        "L0_ffn_wi_W": rng.standard_normal((4096, 1024), dtype=np.float32) * 0.03,
        "L0_ffn_wi_b": rng.standard_normal((4096,), dtype=np.float32) * 0.01,
        "L0_ffn_wo_W": rng.standard_normal((1024, 4096), dtype=np.float32) * 0.015,
        "L0_ffn_wo_b": rng.standard_normal((1024,), dtype=np.float32) * 0.01,
        "L0_ffn_lora_A": rng.standard_normal((8, 4, 1024), dtype=np.float32) * 0.03,
        "L0_ffn_lora_B": rng.standard_normal((8, 4096, 4), dtype=np.float32) * 0.02,
    }
    out = kernel(**fake)
    print("out", out.shape, out.dtype, float(np.abs(out).mean()))


# version 13
# speedup vs baseline: 1.2898x; 1.0054x over previous
import numpy as np
import concourse.bass as bass
import concourse.mybir as mybir
from concourse.bass_utils import run_bass_kernel_spmd

F32 = mybir.dt.float32
BF16 = mybir.dt.bfloat16
AF = mybir.ActivationFunctionType
ALU = mybir.AluOpType
AX = mybir.AxisListType

D = 1024
FF = 4096
KT = 8
FT = 32
TPC = 2048
CH = 256
NCH = TPC // CH
JT = CH // 128
CHA = 512
NCHA = TPC // CHA
JTA = CHA // 128
NT = TPC // 128
EPS = 1e-5


class Buf:
    __slots__ = ("w", "r")

    def __init__(self):
        self.w = None
        self.r = []


class Bld:

    ENG_SECT = {"t": "tensor", "v": "vector", "s": "scalar", "sy": "sync", "g": "gpsimd"}

    def __init__(self):
        self.ops = {e: [] for e in self.ENG_SECT}
        self.n = {k: 0 for k in ("t", "v", "s", "qx", "qw", "qo")}
        self.waited = {}
        self.dummy_fn = {}
        self.dmy_buf = {"v": Buf(), "s": Buf()}

    def _wait(self, eng, dep):
        stream, val = dep
        if stream == eng:
            if eng == "t":
                return
        key = (eng, stream)
        if self.waited.get(key, 0) >= val:
            return
        self.waited[key] = val
        self.ops[eng].append(("wait", stream, val))

    def op(self, eng, fn, reads=(), writes=(), queue=None):
        if eng in ("v", "s") and self.dummy_fn.get(eng) is not None:
            last = (eng, self.n[eng])
            if any(b.w == last for b in reads):
                self.op(eng, self.dummy_fn[eng], reads=(), writes=(self.dmy_buf[eng],))
        for b in reads:
            if b.w is not None:
                self._wait(eng, b.w)
        for b in writes:
            if b.w is not None:
                self._wait(eng, b.w)
            for rd in b.r:
                self._wait(eng, rd)
        if queue is None:
            self.n[eng] += 1
            tok = (eng, self.n[eng])
            self.ops[eng].append(("do", fn, eng, 1))
        else:
            if self.n[queue] > 0:
                self._wait(eng, (queue, 16 * self.n[queue]))
            self.n[queue] += 1
            tok = (queue, 16 * self.n[queue])
            self.ops[eng].append(("do", fn, queue, 16))
        for b in reads:
            b.r.append(tok)
        for b in writes:
            b.w = tok
            b.r = []

    def raw_wait(self, eng, stream, val):
        key = (eng, stream)
        if self.waited.get(key, 0) >= val:
            return
        self.waited[key] = val
        self.ops[eng].append(("wait", stream, val))


def build_nc():
    nc = bass.Bass("TRN2", target_bir_lowering=False)

    xh = nc.dram_tensor("x", [NT, 128, D], F32, kind="ExternalInput")
    wvt_h = nc.dram_tensor("wvt", [128, KT, D], F32, kind="ExternalInput")
    wout_h = nc.dram_tensor("wout", [128, KT, D], F32, kind="ExternalInput")
    bv_h = nc.dram_tensor("bvp", [128, KT], F32, kind="ExternalInput")
    boutb_h = nc.dram_tensor("boutb", [128, D], F32, kind="ExternalInput")
    gwt_h = nc.dram_tensor("gwt", [128, KT, 8], F32, kind="ExternalInput")
    gb_h = nc.dram_tensor("gb", [8, 1], F32, kind="ExternalInput")
    wit_h = nc.dram_tensor("wit", [128, KT, FF], F32, kind="ExternalInput")
    wib_h = nc.dram_tensor("wibp", [128, FT], F32, kind="ExternalInput")
    wot_h = nc.dram_tensor("wot", [128, FT, D], F32, kind="ExternalInput")
    wobb_h = nc.dram_tensor("wobb", [128, D], F32, kind="ExternalInput")
    a2t_h = nc.dram_tensor("a2t", [128, KT, 32], F32, kind="ExternalInput")
    b2_h = nc.dram_tensor("b2", [32, FF], F32, kind="ExternalInput")
    eye_h = nc.dram_tensor("eye", [128, 128], F32, kind="ExternalInput")
    x1s_h = nc.dram_tensor("x1s", [NT, 128, D], F32, kind="Internal")
    yh = nc.dram_tensor("y", [NT, 128, D], F32, kind="ExternalOutput")

    layout = []

    def L(name, shape, dtype):
        layout.append((name, shape, dtype))
        return name

    static = [
        ("xsb0", [128, JTA, D], F32), ("xsb1", [128, JTA, D], F32),
        ("htmp", [128, 2, D], BF16),
        ("hT", [128, KT, CHA], BF16),
        ("trash", [128, D], BF16),
        ("eyebf", [128, 128], BF16),
        ("stats", [128, 80], F32),
        ("lg", [8, CH], BF16),
        ("lgT", [128, 16], BF16),
        ("maskT", [128, JT, 32], BF16),
        ("mask32", [32, CH], F32),
        ("xam", [32, 2 * CH], BF16),
        ("bv", [128, KT], F32),
        ("gateb", [8, 16], F32),
        ("wib", [128, FT], F32),
        ("a2t", [128, KT, 32], BF16),
        ("gwt", [128, KT, 8], BF16),
        ("if0", [128, CH], BF16), ("if1", [128, CH], BF16),
        ("dmyv", [128, 64], F32), ("dmys", [128, 64], F32),
        ("aoraw", [128, JTA, D], F32),
        ("wiT", [128, KT, FF], BF16),
        ("woTlo", [128, 16, D], BF16),
    ]
    dt_size = {F32: 4, BF16: 2}
    off = 0
    offs = {}
    for name, shape, dtype in static:
        per_part = int(np.prod(shape[1:])) * dt_size[dtype]
        offs[name] = off
        off += (per_part + 31) // 32 * 32
    alias_base = off
    alias = [
        ("wvT", [128, KT, D], BF16, 0),
        ("woutT", [128, KT, D], BF16, 16384),
        ("vT", [128, KT, CHA], BF16, 32768),
        ("boutb", [128, D], F32, 40960),
        ("woThi", [128, 16, D], BF16, 0),
        ("wobb", [128, D], F32, 32768),
        ("b2a", [32, FF // 2], BF16, 36864),
        ("b2b", [32, FF // 2], BF16, 40960),
    ]
    total = alias_base + 45056
    arena = nc.alloc_sbuf_tensor("arena", [128, total], mybir.dt.uint8)
    base = nc.lookup_mloc(arena).addr

    sb = {}
    for name, shape, dtype in static:
        sb[name] = nc.alloc_sbuf_tensor_at(name, shape, dtype, offset=base + offs[name])
    for name, shape, dtype, aoff in alias:
        sb[name] = nc.alloc_sbuf_tensor_at(name, shape, dtype, offset=base + alias_base + aoff)

    pT0 = nc.alloc_psum_tensor("pT0", [128, 1024], BF16)
    pV0 = nc.alloc_psum_tensor("pV0", [128, 512], F32)
    pV1 = nc.alloc_psum_tensor("pV1", [128, 512], F32)
    pW2 = nc.alloc_psum_tensor("pW2", [128, 512], F32)
    pO0 = nc.alloc_psum_tensor("pO0", [128, 1024], F32)
    pO1 = nc.alloc_psum_tensor("pO1", [128, 1024], F32)
    pT = [pT0, pT0]
    pV = [pV0, pV1]
    pWI = [pV1, pW2]
    pO = [pO0, pO1]

    B = Bld()
    B.dummy_fn["v"] = lambda v: v.tensor_copy(sb["dmyv"][:, 0:32], sb["dmyv"][:, 32:64])
    B.dummy_fn["s"] = lambda s: s.copy(sb["dmys"][:, 0:32], sb["dmys"][:, 32:64])
    xt = [[Buf() for _ in range(JTA)] for _ in range(2)]
    bhtmp = [Buf(), Buf()]
    bhT2 = [Buf(), Buf()]
    vTb = [Buf() for _ in range(KT)]
    _bpT0 = Buf()
    bpT = [_bpT0, _bpT0]
    bpV = [Buf(), Buf()]
    bpWI = [bpV[1], Buf()]
    bpO = [Buf(), Buf()]
    bwv = [Buf(), Buf()]; bwo = Buf(); bgwt = Buf(); ba2t = Buf(); beye = Buf()
    bb2 = [Buf(), Buf()]
    bwiT = [Buf() for _ in range(4)]
    bwoT = [Buf() for _ in range(4)]
    bbv = Buf(); bboutb = Buf(); bgb = Buf(); bwib = Buf(); bwobb = Buf()
    bst = {k: Buf() for k in ("s", "q", "m", "e", "ms", "v", "sd", "r",
                              "s2", "q2", "m2", "e2", "ms2", "v2", "sd2", "r2", "mx")}
    blg = Buf(); blgT = Buf(); bmaskT = [Buf() for _ in range(JT)]
    bmask32 = Buf(); bxam2 = [Buf(), Buf()]
    bif = [Buf(), Buf()]
    bx1s = [Buf() for _ in range(NT)]
    btrash = Buf()

    SC = {"s": 0, "q": 4, "m": 8, "e": 12, "ms": 16, "v": 20, "sd": 24, "r": 28,
          "s2": 32, "q2": 36, "m2": 40, "e2": 44, "ms2": 48, "v2": 52, "sd2": 56,
          "r2": 60, "mx": 64}

    st = sb["stats"]

    def stc(key, j=0, n=1):
        c = SC[key] + j
        return st[:, c:c + n]

    B.op("v", lambda v: v.memset(sb["dmyv"][:, :], 0.0), writes=[B.dmy_buf["v"]])
    B.op("s", lambda s: s.memzero(sb["dmys"][:, :]), writes=[B.dmy_buf["s"]])

    B.op("g", lambda g: g.dma_start(sb["eyebf"][:, :], eye_h[:, :]), writes=[beye], queue="qw")
    B.op("g", lambda g: g.dma_start(sb["wvT"][:, :, 0:512], wvt_h[:, :, 0:512]),
         writes=[bwv[0]], queue="qw")
    B.op("g", lambda g: g.dma_start(sb["wvT"][:, :, 512:1024], wvt_h[:, :, 512:1024]),
         writes=[bwv[1]], queue="qw")
    B.op("g", lambda g: g.dma_start(sb["woutT"][:, :, :], wout_h[:, :, :]), writes=[bwo], queue="qw")
    B.op("g", lambda g: g.dma_start(sb["gwt"][:, :, :], gwt_h[:, :, :]), writes=[bgwt], queue="qw")
    B.op("g", lambda g: g.dma_start(sb["a2t"][:, :, :], a2t_h[:, :, :]), writes=[ba2t], queue="qw")
    for gidx in range(4):
        k0 = 2 * gidx
        B.op("g", lambda g, k0=k0: g.dma_start(sb["wiT"][:, k0:k0 + 2, :], wit_h[:, k0:k0 + 2, :]),
             writes=[bwiT[gidx]], queue="qw")
    for gidx in range(2):
        f0 = 8 * gidx
        B.op("g", lambda g, f0=f0: g.dma_start(sb["woTlo"][:, f0:f0 + 8, :], wot_h[:, f0:f0 + 8, :]),
             writes=[bwoT[gidx]], queue="qw")
    def load_x(src, c, jt):
        cb = c % 2
        for j in range(jt):
            g = jt * c + j
            eng, q = ("sy", "qx") if j % 2 == 0 else ("s", "qo")
            B.op(eng, lambda s, g=g, cb=cb, j=j: s.dma_start(sb["xsb%d" % cb][:, j, :], src[g]),
                 reads=([bx1s[g]] if src is x1s_h else []), writes=[xt[cb][j]], queue=q)

    load_x(xh, 0, JTA)
    B.op("sy", lambda s: s.dma_start(sb["bv"][:, :], bv_h[:, :]), writes=[bbv], queue="qx")
    B.op("sy", lambda s: s.dma_start(sb["boutb"][:, :], boutb_h[:, :]), writes=[bboutb], queue="qx")
    B.op("sy", lambda s: s.dma_start(sb["gateb"][:, 0:1], gb_h[:, :]), writes=[bgb], queue="qx")
    B.op("sy", lambda s: s.dma_start(sb["wib"][:, :], wib_h[:, :]), writes=[bwib], queue="qx")

    def ln_batched(cb, inp_bufs, jt):
        xs = sb["xsb%d" % cb]
        hm = sb["htmp"]
        for j in range(jt):
            B.op("v", lambda v, j=j, xs=xs: v.reduce_sum(stc("s", j), xs[:, j, :], axis=AX.X),
                 reads=[inp_bufs[j]], writes=[bst["s"]])
        for j in range(jt):
            B.op("s", lambda s, j=j, xs=xs: s.activation(sb["trash"][:, :], xs[:, j, :], AF.Square,
                                                         bias=0.0, scale=1.0, accum_out=stc("q", j)),
                 reads=[inp_bufs[j]], writes=[btrash, bst["q"]])
        B.op("v", lambda v: v.tensor_scalar_mul(stc("m", 0, jt), stc("s", 0, jt), 1.0 / D),
             reads=[bst["s"]], writes=[bst["m"]])
        B.op("v", lambda v: v.tensor_scalar_mul(stc("e", 0, jt), stc("q", 0, jt), 1.0 / D),
             reads=[bst["q"]], writes=[bst["e"]])
        B.op("v", lambda v: v.tensor_mul(stc("ms", 0, jt), stc("m", 0, jt), stc("m", 0, jt)),
             reads=[bst["m"]], writes=[bst["ms"]])
        B.op("v", lambda v: v.scalar_tensor_tensor(stc("v", 0, jt), stc("e", 0, jt), EPS,
                                                   stc("ms", 0, jt), ALU.add, ALU.subtract),
             reads=[bst["e"], bst["ms"]], writes=[bst["v"]])
        B.op("s", lambda s: s.activation(stc("sd", 0, jt), stc("v", 0, jt), AF.Sqrt, bias=0.0, scale=1.0),
             reads=[bst["v"]], writes=[bst["sd"]])
        B.op("v", lambda v: v.reciprocal(stc("r", 0, jt), stc("sd", 0, jt)),
             reads=[bst["sd"]], writes=[bst["r"]])

    def transposes(cb, jt, hto, hbufs):
        xs = sb["xsb%d" % cb]
        hm = sb["htmp"]
        for j in range(jt):
            B.op("v", lambda v, j=j, xs=xs, hm=hm: v.tensor_scalar(
                hm[:, j % 2, :], xs[:, j, :], stc("m", j), stc("r", j), ALU.subtract, ALU.mult),
                reads=[xt[cb][j], bst["m"], bst["r"]], writes=[bhtmp[j % 2]])
            for dt in range(KT):
                B.op("t", lambda t, j=j, dt=dt, hm=hm: t.transpose(
                    pT[0][:, dt * 128:(dt + 1) * 128],
                    hm[:, j % 2, dt * 128:(dt + 1) * 128], sb["eyebf"][:, :]),
                    reads=[bhtmp[j % 2], beye], writes=[bpT[0]])
            B.op("s", lambda s, j=j, hto=hto: s.copy(
                sb["hT"][:, :, hto + j * 128: hto + (j + 1) * 128],
                pT[0][:, :].rearrange("p (a b) -> p a b", b=128)),
                reads=[bpT[0]], writes=hbufs)

    baoraw = [Buf() for _ in range(JTA)]
    baotmp = Buf()

    def epi_quick(pOb, j_slot):
        B.op("s", lambda s, pOb=pOb, j_slot=j_slot: s.copy(sb["aoraw"][:, j_slot, :], pO[pOb][:, :]),
             reads=[bpO[pOb]], writes=[baoraw[j_slot]])

    def epi_batch(cb, jt, bias_name, bias_buf, store_dst, g0, sq_on_vector):
        xs = sb["xsb%d" % cb]
        ao = sb["aoraw"]
        for j in range(jt):
            B.op("v", lambda v, j=j, ao=ao: v.scalar_tensor_tensor(
                ao[:, j, :], ao[:, j, :], 1.0, sb[bias_name][:, :],
                ALU.mult, ALU.add, accum_out=stc("s2", j)),
                reads=[baoraw[j], bias_buf], writes=[baoraw[j], bst["s2"]])
        for j in range(jt):
            if sq_on_vector:
                B.op("v", lambda v, j=j, ao=ao: v.scalar_tensor_tensor(
                    sb["trash"][:, :], ao[:, j, :], 1.0, ao[:, j, :],
                    ALU.mult, ALU.mult, accum_out=stc("q2", j)),
                    reads=[baoraw[j]], writes=[btrash, bst["q2"]])
            else:
                B.op("s", lambda s, j=j, ao=ao: s.activation(
                    sb["trash"][:, :], ao[:, j, :], AF.Square,
                    bias=0.0, scale=1.0, accum_out=stc("q2", j)),
                    reads=[baoraw[j]], writes=[btrash, bst["q2"]])
        B.op("v", lambda v: v.tensor_scalar_mul(stc("m2", 0, jt), stc("s2", 0, jt), 1.0 / D),
             reads=[bst["s2"]], writes=[bst["m2"]])
        B.op("v", lambda v: v.tensor_scalar_mul(stc("e2", 0, jt), stc("q2", 0, jt), 1.0 / D),
             reads=[bst["q2"]], writes=[bst["e2"]])
        B.op("v", lambda v: v.tensor_mul(stc("ms2", 0, jt), stc("m2", 0, jt), stc("m2", 0, jt)),
             reads=[bst["m2"]], writes=[bst["ms2"]])
        B.op("v", lambda v: v.scalar_tensor_tensor(stc("v2", 0, jt), stc("e2", 0, jt), EPS,
                                                   stc("ms2", 0, jt), ALU.add, ALU.subtract),
             reads=[bst["e2"], bst["ms2"]], writes=[bst["v2"]])
        B.op("s", lambda s: s.activation(stc("sd2", 0, jt), stc("v2", 0, jt), AF.Sqrt,
                                         bias=0.0, scale=1.0),
             reads=[bst["v2"]], writes=[bst["sd2"]])
        B.op("v", lambda v: v.reciprocal(stc("r2", 0, jt), stc("sd2", 0, jt)),
             reads=[bst["sd2"]], writes=[bst["r2"]])
        for j in range(jt):
            B.op("v", lambda v, j=j, ao=ao: v.tensor_scalar(
                ao[:, j, :], ao[:, j, :], stc("m2", j), stc("r2", j), ALU.subtract, ALU.mult),
                reads=[baoraw[j], bst["m2"], bst["r2"]], writes=[baoraw[j]])
            B.op("v", lambda v, j=j, xs=xs, ao=ao: v.tensor_add(xs[:, j, :], xs[:, j, :], ao[:, j, :]),
                 reads=[xt[cb][j], baoraw[j]], writes=[xt[cb][j]])
            g = g0 + j
            extra = [bx1s[g]] if store_dst is x1s_h else []
            if j % 2 == 0:
                B.op("sy", lambda s, g=g, j=j, xs=xs: s.dma_start(store_dst[g], xs[:, j, :]),
                     reads=[xt[cb][j]], writes=extra, queue="qx")
            else:
                B.op("s", lambda s, g=g, j=j, xs=xs: s.dma_start(store_dst[g], xs[:, j, :]),
                     reads=[xt[cb][j]], writes=extra, queue="qo")

    def a_prep(c):
        cb = c % 2
        ln_batched(cb, xt[cb], JTA)
        transposes(cb, JTA, 0, bhT2)

    a_prep(0)
    for c in range(NCHA):
        cb = c % 2
        for dt in range(KT):
            pb = dt % 2
            for k in range(KT):
                B.op("t", lambda t, pb=pb, k=k, dt=dt: t.matmul(
                    pV[pb][:, 0:CHA], sb["wvT"][:, k, dt * 128:(dt + 1) * 128], sb["hT"][:, k, :],
                    start=(k == 0), stop=(k == KT - 1)),
                    reads=[bwv[dt // 4]] + bhT2, writes=[bpV[pb]])
            B.op("s", lambda s, pb=pb, dt=dt: s.activation(
                sb["vT"][:, dt, :], pV[pb][:, 0:CHA], AF.Identity,
                bias=sb["bv"][:, dt:dt + 1], scale=1.0),
                reads=[bpV[pb], bbv], writes=[vTb[dt]])
        if c >= 1:
            epi_batch((c - 1) % 2, JTA, "boutb", bboutb, x1s_h, JTA * (c - 1), sq_on_vector=False)
        if c + 1 < NCHA:
            load_x(xh, c + 1, JTA)
        for j in range(JTA):
            for nch in range(2):
                for k in range(KT):
                    B.op("t", lambda t, j=j, nch=nch, k=k: t.matmul(
                        pO[j % 2][:, nch * 512:(nch + 1) * 512],
                        sb["vT"][:, k, j * 128:(j + 1) * 128],
                        sb["woutT"][:, k, nch * 512:(nch + 1) * 512],
                        start=(k == 0), stop=(k == KT - 1)),
                        reads=[vTb[k], bwo], writes=[bpO[j % 2]])
            epi_quick(j % 2, j)
            if j == 1 and c + 1 < NCHA:
                a_prep(c + 1)
    epi_batch((NCHA - 1) % 2, JTA, "boutb", bboutb, x1s_h, JTA * (NCHA - 1), sq_on_vector=False)

    B.op("g", lambda g: g.dma_start(sb["woThi"][:, 0:8, :], wot_h[:, 16:24, :]),
         writes=[bwoT[2]] + bwv, queue="qw")
    B.op("g", lambda g: g.dma_start(sb["woThi"][:, 8:16, :], wot_h[:, 24:32, :]),
         writes=[bwoT[3], bwo], queue="qw")
    B.op("g", lambda g: g.dma_start(sb["b2a"][:, :], b2_h[:, 0:FF // 2]),
         writes=[bb2[0]] + vTb, queue="qw")
    B.op("g", lambda g: g.dma_start(sb["b2b"][:, :], b2_h[:, FF // 2:FF]),
         writes=[bb2[1], bboutb] + vTb, queue="qw")
    B.op("sy", lambda s: s.dma_start(sb["wobb"][:, :], wobb_h[:, :]),
         writes=[bwobb] + vTb, queue="qx")

    def b_prep(c):
        cb = c % 2
        hto = cb * CH
        hb = [bhT2[cb]]
        ln_batched(cb, xt[cb], JT)
        transposes(cb, JT, hto, hb)
        for k in range(KT):
            B.op("t", lambda t, k=k, hto=hto: t.matmul(
                pV[0][0:8, 0:CH], sb["gwt"][:, k, :], sb["hT"][:, k, hto:hto + CH],
                start=(k == 0), stop=(k == KT - 1)),
                reads=[bgwt] + hb, writes=[bpV[0]])
        for k in range(KT):
            B.op("t", lambda t, k=k, hto=hto: t.matmul(
                pV[0][0:32, 256:256 + CH], sb["a2t"][:, k, :],
                sb["hT"][:, k, hto:hto + CH], start=(k == 0), stop=(k == KT - 1)),
                reads=[ba2t] + hb, writes=[bpV[0]])
        B.op("s", lambda s: s.activation(sb["lg"][:, :], pV[0][0:8, 0:CH], AF.Identity,
                                         bias=sb["gateb"][:, 0:1], scale=1.0),
             reads=[bpV[0], bgb], writes=[blg])
        for j in range(JT):
            B.op("t", lambda t, j=j: t.transpose(pT[0][:, j * 8:(j + 1) * 8],
                                                 sb["lg"][:, j * 128:(j + 1) * 128],
                                                 sb["eyebf"][0:8, 0:8]),
                 reads=[blg, beye], writes=[bpT[0]])
            B.op("s", lambda s, j=j: s.copy(sb["lgT"][:, j * 8:(j + 1) * 8], pT[0][:, j * 8:(j + 1) * 8]),
                 reads=[bpT[0]], writes=[blgT])
            B.op("v", lambda v, j=j: v.reduce_max(stc("mx", j), sb["lgT"][:, j * 8:(j + 1) * 8], axis=AX.X),
                 reads=[blgT], writes=[bst["mx"]])
            for r in range(4):
                B.op("v", lambda v, j=j, r=r: v.tensor_scalar(
                    sb["maskT"][:, j, r * 8:(r + 1) * 8], sb["lgT"][:, j * 8:(j + 1) * 8],
                    stc("mx", j), None, ALU.is_ge),
                    reads=[blgT, bst["mx"]], writes=[bmaskT[j]])
            B.op("t", lambda t, j=j: t.transpose(pT[0][0:32, 128 + j * 128:256 + j * 128],
                                                 sb["maskT"][:, j, :], sb["eyebf"][:, :]),
                 reads=[bmaskT[j], beye], writes=[bpT[0]])
            B.op("s", lambda s, j=j: s.copy(sb["mask32"][0:32, j * 128:(j + 1) * 128],
                                            pT[0][0:32, 128 + j * 128:256 + j * 128]),
                 reads=[bpT[0]], writes=[bmask32])
        B.op("v", lambda v, hto=hto: v.tensor_mul(sb["xam"][0:32, hto:hto + CH],
                                                  pV[0][0:32, 256:256 + CH],
                                                  sb["mask32"][0:32, :]),
             reads=[bpV[0], bmask32], writes=[bxam2[cb]])

    load_x(x1s_h, 0, JT)
    b_prep(0)
    for c in range(NCH):
        cb = c % 2
        hto = cb * CH
        hb = [bhT2[cb]]
        xo = cb * CH

        def wo_emit(ft):
            ib = ft % 2
            wo_sb = sb["woTlo"] if ft < 16 else sb["woThi"]
            fl = ft if ft < 16 else ft - 16
            for j in range(JT):
                for nch in range(2):
                    B.op("t", lambda t, ib=ib, j=j, nch=nch, wo_sb=wo_sb, fl=fl, ft=ft: t.matmul(
                        pO[j][:, nch * 512:(nch + 1) * 512],
                        sb["if%d" % ib][:, j * 128:(j + 1) * 128],
                        wo_sb[:, fl, nch * 512:(nch + 1) * 512],
                        start=(ft == 0), stop=(ft == FT - 1)),
                        reads=[bif[ib], bwoT[ft // 8]], writes=[bpO[j]])

        for ft in range(FT):
            if ft == 8:
                if c >= 1:
                    epi_batch((c - 1) % 2, JT, "wobb", bwobb, yh, JT * (c - 1), sq_on_vector=True)
                if c + 1 < NCH:
                    load_x(x1s_h, c + 1, JT)
            if ft == 16 and c + 1 < NCH:
                b_prep(c + 1)
            ib = ft % 2
            pw = pWI[ib]
            bpw = bpWI[ib]
            for k in range(KT):
                B.op("t", lambda t, k=k, ft=ft, pw=pw, hto=hto: t.matmul(
                    pw[:, 0:CH], sb["wiT"][:, k, ft * 128:(ft + 1) * 128], sb["hT"][:, k, hto:hto + CH],
                    start=(k == 0), stop=False),
                    reads=[bwiT[k // 2]] + hb, writes=[bpw])
            b2t = sb["b2a"] if ft < 16 else sb["b2b"]
            fb = ft if ft < 16 else ft - 16
            B.op("t", lambda t, pw=pw, b2t=b2t, fb=fb, xo=xo: t.matmul(
                pw[:, 0:CH], b2t[0:32, fb * 128:(fb + 1) * 128],
                sb["xam"][0:32, xo:xo + CH], start=False, stop=True),
                 reads=[bb2[0 if ft < 16 else 1], bxam2[cb]], writes=[bpw])
            B.op("s", lambda s, ib=ib, ft=ft, pw=pw: s.activation(
                sb["if%d" % ib][:, :], pw[:, 0:CH], AF.Relu,
                bias=sb["wib"][:, ft:ft + 1], scale=1.0),
                reads=[bpw, bwib], writes=[bif[ib]])
            if ft >= 1:
                wo_emit(ft - 1)
        wo_emit(FT - 1)
        for j in range(JT):
            epi_quick(j, j)
    epi_batch((NCH - 1) % 2, JT, "wobb", bwobb, yh, JT * (NCH - 1), sq_on_vector=True)

    B.raw_wait("sy", "qx", 16 * B.n["qx"])
    B.raw_wait("s", "qo", 16 * B.n["qo"])

    with (
        nc.semaphore("sem_t") as sem_t,
        nc.semaphore("sem_v") as sem_v,
        nc.semaphore("sem_s") as sem_s,
        nc.semaphore("sem_qx") as sem_qx,
        nc.semaphore("sem_qw") as sem_qw,
        nc.semaphore("sem_qo") as sem_qo,
        nc.Block() as block,
    ):
        sems = {"t": sem_t, "v": sem_v, "s": sem_s, "qx": sem_qx, "qw": sem_qw, "qo": sem_qo}

        def replay(eng_key):
            def run(h):
                for item in B.ops[eng_key]:
                    if item[0] == "wait":
                        h.wait_ge(sems[item[1]], item[2])
                    else:
                        _, fn, semkey, inc = item
                        fn(h).then_inc(sems[semkey], inc)
            return run

        @block.tensor
        def _(t):
            replay("t")(t)

        @block.vector
        def _(v):
            replay("v")(v)

        @block.scalar
        def _(s):
            replay("s")(s)

        @block.sync
        def _(sy):
            replay("sy")(sy)

        @block.gpsimd
        def _(g):
            replay("g")(g)

    return nc


def _prep(inp):
    w_in = inp["L0_attn_w_in"]
    b_in = inp["L0_attn_b_in"]
    w_out = inp["L0_attn_w_out"]
    b_out = inp["L0_attn_b_out"]
    gate_W = inp["L0_ffn_gate_W"]
    gate_b = inp["L0_ffn_gate_b"]
    wi_W = inp["L0_ffn_wi_W"]
    wi_b = inp["L0_ffn_wi_b"]
    wo_W = inp["L0_ffn_wo_W"]
    wo_b = inp["L0_ffn_wo_b"]
    lora_A = inp["L0_ffn_lora_A"]
    lora_B = inp["L0_ffn_lora_B"]

    A = np.ascontiguousarray
    wv = w_in.reshape(16, 3, 64, D)[:, 2].reshape(D, D)
    d = {
        "wvt": A(wv.T.reshape(KT, 128, D).transpose(1, 0, 2)),
        "wout": A(w_out.T.reshape(KT, 128, D).transpose(1, 0, 2)),
        "bvp": A(b_in.reshape(16, 3, 64)[:, 2].reshape(KT, 128).T),
        "boutb": A(np.broadcast_to(b_out, (128, D))),
        "gwt": A(gate_W.T.reshape(KT, 128, 8).transpose(1, 0, 2)),
        "gb": A(gate_b.reshape(8, 1)),
        "wit": A(wi_W.T.reshape(KT, 128, FF).transpose(1, 0, 2)),
        "wibp": A(wi_b.reshape(FT, 128).T),
        "wot": A(wo_W.T.reshape(FT, 128, D).transpose(1, 0, 2)),
        "wobb": A(np.broadcast_to(wo_b, (128, D))),
        "a2t": A(lora_A.transpose(1, 0, 2).reshape(32, D).T.reshape(KT, 128, 32).transpose(1, 0, 2)),
        "b2": A(lora_B.transpose(2, 0, 1).reshape(32, FF)),
        "eye": np.eye(128, dtype=np.float32),
    }
    return {k: np.asarray(v, dtype=np.float32) for k, v in d.items()}


def kernel(**inputs):
    inputs = {k: np.asarray(v) for k, v in inputs.items()}
    x = np.ascontiguousarray(inputs["L0_attn_x"], dtype=np.float32)
    xf = x.reshape(8 * 2048, D)
    w = _prep(inputs)
    nc = build_nc()
    in_maps = []
    for c in range(8):
        m = dict(w)
        m["x"] = np.ascontiguousarray(xf[c * TPC:(c + 1) * TPC].reshape(NT, 128, D))
        in_maps.append(m)
    res = run_bass_kernel_spmd(nc, in_maps, core_ids=list(range(8)))
    out = np.concatenate([r["y"].reshape(TPC, D) for r in res.results], axis=0)
    return out.reshape(8, 2048, D).astype(np.float32)


if __name__ == "__main__":
    rng = np.random.default_rng(0)
    fake = {
        "L0_attn_x": rng.standard_normal((8, 2048, 1024), dtype=np.float32),
        "L0_attn_w_in": rng.standard_normal((3072, 1024), dtype=np.float32) * 0.02,
        "L0_attn_b_in": rng.standard_normal((3072,), dtype=np.float32) * 0.02,
        "L0_attn_w_out": rng.standard_normal((1024, 1024), dtype=np.float32) * 0.02,
        "L0_attn_b_out": rng.standard_normal((1024,), dtype=np.float32) * 0.02,
        "L0_ffn_gate_W": rng.standard_normal((8, 1024), dtype=np.float32) * 0.03,
        "L0_ffn_gate_b": rng.standard_normal((8,), dtype=np.float32) * 0.01,
        "L0_ffn_wi_W": rng.standard_normal((4096, 1024), dtype=np.float32) * 0.03,
        "L0_ffn_wi_b": rng.standard_normal((4096,), dtype=np.float32) * 0.01,
        "L0_ffn_wo_W": rng.standard_normal((1024, 4096), dtype=np.float32) * 0.015,
        "L0_ffn_wo_b": rng.standard_normal((1024,), dtype=np.float32) * 0.01,
        "L0_ffn_lora_A": rng.standard_normal((8, 4, 1024), dtype=np.float32) * 0.03,
        "L0_ffn_lora_B": rng.standard_normal((8, 4096, 4), dtype=np.float32) * 0.02,
    }
    out = kernel(**fake)
    print("out", out.shape, out.dtype, float(np.abs(out).mean()))
